# version 15
# baseline (speedup 1.0000x reference)
import contextlib
import numpy as np
import ml_dtypes
import concourse.bass as bass
import concourse.mybir as mybir
from concourse.bass_utils import run_bass_kernel_spmd
from concourse.bass import IndirectOffsetOnAxis
from concourse.alu_op_type import AluOpType as Op

F32 = mybir.dt.float32
F16 = mybir.dt.float16
BF16 = mybir.dt.bfloat16
U32 = mybir.dt.uint32
AF = mybir.ActivationFunctionType

NCORES = 8
B_, N_, D_ = 4, 4096, 1024
H, DK, DV = 4, 256, 512
VDIM = 2048
NT = 2048
TIL = 512
NTIL = NT // TIL
CH = 256
E2 = 2
CAP = 512
FE = 2048
FS = 2048
EPS = 1e-5
ROPE_BASE = 10000.0
BIS_IT = 24
KB = 1024

GAMMA = 1.0 - np.exp2(-5.0 - np.arange(H).astype(np.float64))
GC = [float(g ** CH) for g in GAMMA]


def _host_shared():
    d = {}
    d["ident"] = np.eye(128, dtype=np.float32)
    inv_freq = 1.0 / ROPE_BASE ** (np.arange(0, DK, 2, dtype=np.float64) / DK)
    t = np.arange(N_, dtype=np.float64)
    fr = np.outer(inv_freq, t)
    d["cos_full"] = np.cos(fr).astype(np.float32)
    d["sin_full"] = np.sin(fr).astype(np.float32)
    dm = np.zeros((128, H, 2, CH), dtype=np.float32)
    for h in range(H):
        g = GAMMA[h]
        i = np.arange(CH)[None, :]
        for jc in range(2):
            j = (jc * 128 + np.arange(128))[:, None]
            rel = i - j
            dm[:, h, jc, :] = np.where(rel >= 0, g ** np.maximum(rel, 0), 0.0) / 16.0
    d["dmaskT"] = dm.reshape(128, H * 2 * CH).copy()
    ck = np.zeros((128, H, 2), dtype=np.float32)
    cq = np.zeros((128, H, 2), dtype=np.float32)
    for h in range(H):
        g = GAMMA[h]
        for sub in range(2):
            pos = sub * 128 + np.arange(128)
            ck[:, h, sub] = g ** (CH - 1 - pos)
            cq[:, h, sub] = (g ** (pos + 1)) / 16.0
    d["crossk"] = ck.reshape(128, H * 2).copy()
    d["crossq"] = cq.reshape(128, H * 2).copy()
    d["ones16"] = np.ones((16, 1), dtype=np.float32)
    fv = np.zeros((128, 8), dtype=np.float32)
    for p in range(128):
        for j in range(8):
            fv[p, j] = (j + 1) if p == 0 else ((4 * p + j + 1) if j < 4 else (j + 1))
    d["findvals"] = fv
    return d


def _host_percore(c):
    b, hf = c // 2, c % 2
    d = {}
    d["gate"] = np.full((128, 1), 1.0 if hf == 1 else 0.0, dtype=np.float32)
    partner = c - 1 if hf == 1 else c
    d["sinit_rows"] = (partner * 1024 + (np.arange(8)[None, :] * 128
                                         + np.arange(128)[:, None])).astype(np.uint32)
    er = np.zeros((8, 2), np.uint32)
    dr = np.zeros((8, 2), np.uint32)
    for ei in range(E2):
        e = 2 * c + ei
        for bb in range(4):
            g = 4 * ei + bb
            for hh in range(2):
                er[g, hh] = 16 * (2 * bb + hh) + e
                dr[g, hh] = 2 * bb + hh
    d["explog_rows"] = er
    d["den_rows"] = dr
    eor = np.zeros((128, 64), np.uint32)
    idxr = np.zeros((128, 16), np.uint32)
    p = np.arange(128)
    s = 0
    for sc in range(8):
        for ei in range(E2):
            g = 4 * ei + b
            for j in range(4):
                eor[:, s * 4 + j] = (sc * 8 + g) * CAP + j * 128 + p
            idxr[:, s] = (sc * 8 + g) * 128 + p
            s += 1
    d["eo_rows"] = eor
    d["idx_rows"] = idxr
    d["acc_rows"] = (2048 * hf + (np.arange(16)[None, :] * 128
                                  + p[:, None])).astype(np.uint32)
    return d


class Sem:
    def __init__(self, h):
        self.h = h
        self.n = 0

    def bump(self, k=1):
        self.n += k
        return self.n


class MK:
    ENGS = ("sync", "vector", "scalar", "tensor", "gpsimd")

    def __init__(self, nc, arena_pb):
        self.nc = nc
        self.q = {e: [] for e in self.ENGS}
        self.arena_pb = arena_pb
        self.overlays = []
        self.res = {}

    def op(self, eng, f, inc=None, k=1):
        if inc is not None:
            val = inc.bump(k)
            self.q[eng].append(lambda E, f=f, s=inc.h, k=k: f(E).then_inc(s, k))
            return val
        self.q[eng].append(f)
        return None

    def wait(self, eng, sem, val):
        assert val <= sem.n, f"wait on future value {val} > {sem.n}"
        if val > 0:
            self.q[eng].append(lambda E, s=sem.h, v=val: E.wait_ge(s, v))

    def use(self, eng, *names):
        for nm in names:
            s = self.res.get(nm)
            if s is not None:
                self.wait(eng, s[0], s[1])

    def free(self, nm, sem, val=None):
        self.res[nm] = (sem, sem.n if val is None else val)

    def sb(self, name, shape, dtype, off_b):
        nbytes = int(np.prod(shape[1:])) * mybir.dt.size(dtype)
        assert off_b % 32 == 0 and off_b + nbytes <= self.arena_pb, (name, off_b, nbytes)
        h = self.nc.virtual_sbuf_tensor(name, list(shape), dtype)
        self.overlays.append((h, off_b))
        return h

    def finalize(self, arena_handle):
        base = self.nc.lookup_mloc(arena_handle).addr
        for h, off in self.overlays:
            self.nc._tensor(h.name, h.shape, h.dtype, type="SB")
            mloc = self.nc.lookup_mloc(h)
            mloc.allocated = True
            mloc.addr = base + off


ARENA_PB = 176 * KB
A2B_SUB = 9


def build_nc(upto="full", dbg=()):
    dbg = set(dbg)
    PHASES = ["p1", "p2", "a2b", "broute", "c", "d", "full"]
    lim = PHASES.index(upto)

    def on(ph):
        return PHASES.index(ph) <= lim

    nc = bass.Bass("TRN2", target_bir_lowering=False)

    def din(name, shape, dt=F32):
        return nc.dram_tensor(name, list(shape), dt, kind="ExternalInput").ap()

    def dint(name, shape, dt=F32):
        return nc.dram_tensor(name, list(shape), dt).ap()

    x_loc = din("x_loc", [NT, D_])
    Wq_i = din("Wq", [D_, D_]); Wk_i = din("Wk", [D_, D_])
    Wv_i = din("Wv", [D_, VDIM]); Wg_i = din("Wg", [D_, VDIM]); Wo_i = din("Wo", [VDIM, D_])
    gn_i = din("gn_bc", [128, DV])
    gatewT_i = din("gatewT", [D_, 16])
    WsgT_i = din("WsgT", [D_, FS]); WsuT_i = din("WsuT", [D_, FS]); WsdT_i = din("WsdT", [FS, D_])
    WgTe_i = din("WgTe", [E2, D_, FE]); WuTe_i = din("WuTe", [E2, D_, FE])
    WdTe_i = din("WdTe", [E2, FE, D_])
    ident_i = din("ident", [128, 128])
    cos_i = din("cos_t", [128, NT]); sin_i = din("sin_t", [128, NT])
    dmask_i = din("dmaskT", [128, H * 2 * CH])
    crossk_i = din("crossk", [128, H * 2]); crossq_i = din("crossq", [128, H * 2])
    ones16_i = din("ones16", [16, 1])
    findv_i = din("findvals", [128, 8])
    gate_i = din("gate", [128, 1])
    sinitr_i = din("sinit_rows", [128, 8], U32)
    elr_i = din("explog_rows", [8, 2], U32)
    denr_i = din("den_rows", [8, 2], U32)
    eor_i = din("eo_rows", [128, 64], U32)
    idxr_i = din("idx_rows", [128, 16], U32)
    accr_i = din("acc_rows", [128, 16], U32)

    out_loc = nc.dram_tensor("out_loc", [NT, D_], F32, kind="ExternalOutput").ap()
    dbg_t = {}

    def dout(name, shape, dt=F32):
        ap = nc.dram_tensor(name, list(shape), dt, kind="ExternalOutput").ap()
        dbg_t[name] = ap
        return ap

    spill_v = dint("spill_v", [NT, VDIM], BF16)
    spill_kc = dint("spill_kc", [H, NT, DK], BF16)
    spill_kr = dint("spill_kr", [H * DK, NT], BF16)
    spill_on = dint("spill_on", [NT, VDIM], BF16)
    ag_s_in = dint("ag_s_in", [1024, DV])
    ag_s_out = dint("ag_s_out", [NCORES * 1024, DV])
    x1b_in = dint("x1b_in", [NT, D_], BF16)
    x1all = dint("x1all", [NCORES * NT, D_], BF16)
    x1f_d = dint("x1f", [NT, D_])
    explog_in = dint("explog_in", [16, NT])
    explog_out = dint("explog_out", [8 * 16, NT])
    den_in = dint("den_in", [1, NT])
    den_out = dint("den_out", [8, NT])
    scores_d = dint("scores_d", [8 * N_, 1])
    rcomb_d = dint("rcomb_d", [8, N_], F16)
    acc_d = dint("acc", [N_, D_])
    eo_in = dint("eo_in", [8 * CAP, D_], BF16)
    eo_all = dint("eo_all", [NCORES * 8 * CAP, D_], BF16)
    idxt_in = dint("idxt_in", [8 * 128, 4], U32)
    idxt_all = dint("idxt_all", [NCORES * 8 * 128, 4], U32)

    es = contextlib.ExitStack()
    sb = lambda n, s, d: es.enter_context(nc.sbuf_tensor(n, list(s), d))

    ident_b = sb("ident_b", [128, 128], BF16)
    cossin = sb("cossin", [128, 2, NT], F16)
    dmaskT = sb("dmaskT_s", [128, H, 2, CH], BF16)
    crossk = sb("crossk_s", [128, H, 2], F32)
    crossq = sb("crossq_s", [128, H, 2], F32)
    gn_b = sb("gn_b", [128, DV], F32)
    gate_s = sb("gate_s", [128, 1], F32)
    ones16 = sb("ones16_s", [16, 1], F32)
    findv = sb("findv", [128, 8], F16)
    idxr_s = sb("idxr_s", [128, 16], U32)
    sinitr = sb("sinitr", [128, 8], U32)
    eor_s = sb("eor_s", [128, 64], U32)
    accr_s = sb("accr_s", [128, 16], U32)
    elr_s = sb("elr_s", [8, 2], U32)
    denr_s = sb("denr_s", [8, 2], U32)
    osc = sb("osc", [128, 16], F32)
    wgate = sb("wgate", [128, 8, 16], BF16)
    idxtab = sb("idxtab", [128, 8, 8], U32)
    wtab = sb("wtab", [128, 8, 4], F32)
    sc8 = sb("sc8", [8, 12], F32)
    explg = sb("explg", [33, 512], F32)
    arena = sb("arena", [128, ARENA_PB], mybir.dt.uint8)

    P = [es.enter_context(nc.psum_tensor(f"P{i}", [128, 512], F32)) for i in range(6)]
    PT0 = es.enter_context(nc.psum_tensor("PT0", [128, 1024], BF16))
    PT1 = es.enter_context(nc.psum_tensor("PT1", [128, 1024], BF16))
    PTs = [PT0, PT1]

    sems = {k: es.enter_context(nc.semaphore("s_" + k))
            for k in ("dx", "do", "dw", "dg", "v", "s", "t", "cc")}

    M = MK(nc, ARENA_PB)
    dx, do, dw, dg = Sem(sems["dx"]), Sem(sems["do"]), Sem(sems["dw"]), Sem(sems["dg"])
    sv, ss, st, cc = Sem(sems["v"]), Sem(sems["s"]), Sem(sems["t"]), Sem(sems["cc"])

    xfm = M.sb("xfm", [128, 8, NT], BF16, 0)
    x1btm = M.sb("x1btm", [128, 2, 1024], BF16, 0)
    x1fmb = M.sb("x1fmb", [128, 8, 256], BF16, 4 * KB)
    hsb = M.sb("hsb", [128, 16, 256], BF16, 8 * KB)
    x1frel = M.sb("x1frel", [128, 2, 1024], F32, 16 * KB)
    mdump = M.sb("mdump", [8, N_], F16, 24 * KB)
    xtok = [M.sb(f"xtok{j}", [128, 1024], BF16, j * 2 * KB) for j in range(2)]
    xtokfm = M.sb("xtokfm", [128, 8, 512], BF16, 4 * KB)
    hfm = M.sb("hfm", [128, 16, 512], BF16, 12 * KB)
    eotile = M.sb("eotile", [128, 1024], BF16, 28 * KB)
    eoD = [M.sb(f"eoD{j}", [128, 1024], BF16, j * 2 * KB) for j in range(8)]
    idxsl = M.sb("idxsl", [128, 8], U32, 16 * KB)
    accf = M.sb("accf", [128, 1024], F32, 18 * KB)
    outt = M.sb("outt", [128, 1024], F32, 22 * KB)
    lnscr = M.sb("lnscr", [128, 1024], BF16, 26 * KB)
    WkA = M.sb("WkA", [128, 8, 1024], BF16, 32 * KB)
    ogat_b = M.sb("ogat_b", [128, 4, VDIM], BF16, 32 * KB)
    rrep = M.sb("rrep", [128, 4096], F16, 32 * KB)
    WvB = M.sb("WvB", [128, 8, VDIM], BF16, 48 * KB)
    ogat = M.sb("ogat", [128, 4, VDIM], BF16, 48 * KB)
    attnT = M.sb("attnT", [128, 2, CH], BF16, 64 * KB)
    obuf = M.sb("obuf", [128, DV], F32, 65 * KB)
    osq = M.sb("osq", [128, DV], F32, 67 * KB)
    WgB = M.sb("WgB", [128, 8, VDIM], BF16, 48 * KB)
    WsgT = M.sb("WsgT", [128, 8, FS], BF16, 48 * KB)
    WgTe = M.sb("WgTe", [128, 8, FE], BF16, 48 * KB)
    Sm = M.sb("Sm", [128, H, 2, DV], F32, 80 * KB)
    Sb = M.sb("Sb", [128, H, 2, DV], BF16, 96 * KB)
    WoC = M.sb("WoC", [128, 16, 1024], BF16, 80 * KB)
    WsuT = M.sb("WsuT", [128, 8, FS], BF16, 80 * KB)
    WuTe = M.sb("WuTe", [128, 8, FE], BF16, 80 * KB)
    xin = M.sb("xin", [128, 4, 1024], F32, 112 * KB)
    ktile = M.sb("ktile", [128, 8, TIL], BF16, 128 * KB)
    vtile = M.sb("vtile", [128, 4, VDIM], BF16, 136 * KB)
    kctile = M.sb("kctile", [128, 4, H, 256], BF16, 152 * KB)
    xbf = M.sb("xbf", [128, 4, 1024], BF16, 160 * KB)
    rope1 = M.sb("rope1", [128, TIL], F32, 168 * KB)
    rope2 = M.sb("rope2", [128, TIL], F32, 170 * KB)
    krt2 = M.sb("krt2", [128, 8, TIL], BF16, 112 * KB)
    ogat2 = M.sb("ogat2", [128, 4, VDIM], BF16, 128 * KB)
    ogfm = M.sb("ogfm", [128, 16, TIL], BF16, 144 * KB)
    x1pre = M.sb("x1pre", [128, 1024], F32, 160 * KB)
    x1btile = M.sb("x1btile", [128, 1024], BF16, 164 * KB)
    x1fmt = M.sb("x1fmt", [128, 8, TIL], BF16, 166 * KB)
    osqb = M.sb("osqb", [128, 1024], BF16, 174 * KB)
    WsdT = M.sb("WsdT", [128, 16, 1024], BF16, 112 * KB)
    scoresR = M.sb("scoresR", [8, N_], F32, 144 * KB)
    denT = M.sb("denT", [8, N_], F32, 160 * KB)
    mA = M.sb("mA", [8, N_], F16, 160 * KB)
    mB = M.sb("mB", [8, N_], F16, 168 * KB)
    onesS = M.sb("onesS", [8, N_], F16, 144 * KB)
    rA = M.sb("rA", [8, N_], F16, 152 * KB)
    WdTe = M.sb("WdTe", [128, 16, 1024], BF16, 112 * KB)

    with nc.Block() as block:
        M.finalize(arena)

        plain = [(crossk[:, :, :].rearrange("p a b -> p (a b)"), crossk_i[:, :]),
                 (crossq[:, :, :].rearrange("p a b -> p (a b)"), crossq_i[:, :]),
                 (gn_b[:, :], gn_i[:, :]), (gate_s[:, :], gate_i[:, :]),
                 (idxr_s[:, :], idxr_i[:, :]), (sinitr[:, :], sinitr_i[:, :]),
                 (eor_s[:, :], eor_i[:, :]), (accr_s[:, :], accr_i[:, :]),
                 (elr_s[:, :], elr_i[:, :]), (denr_s[:, :], denr_i[:, :]),
                 (ones16[:, :], ones16_i[:, :])]
        for d, s in plain:
            M.op("sync", lambda E, d=d, s=s: E.dma_start(out=d, in_=s), inc=dx, k=16)
        casts = [
            (ident_b[:, :], ident_i[:, :]),
            (dmaskT[:, :, :, :].rearrange("p a b c -> p (a b c)"), dmask_i[:, :]),
            (cossin[:, 0, :], cos_i[:, :]),
            (cossin[:, 1, :], sin_i[:, :]),
            (findv[:, :], findv_i[:, :]),
            (wgate[:, :, :], gatewT_i[:, :].rearrange("(a p) b -> p a b", p=128)),
            (WkA[:, :, :], Wk_i[:, :].rearrange("(a p) b -> p a b", p=128)),
            (WvB[:, :, :], Wv_i[:, :].rearrange("(a p) b -> p a b", p=128)),
        ]
        for d, s in casts:
            M.op("gpsimd", lambda E, d=d, s=s: E.dma_start(out=d, in_=s), inc=dw, k=16)
        DW_P1 = dw.n
        DX_CONST = dx.n
        M.wait("vector", dx, DX_CONST)
        M.wait("vector", dw, DW_P1)
        M.op("vector", lambda E: E.memset(Sm[:, :, :, :].rearrange("p a b c -> p (a b c)"), 0.0), inc=sv)
        SM_ZERO = sv.n

        def emit_xfm_tile(tt):
            T0 = tt * TIL
            M.use("sync", "xin")
            M.op("sync", lambda E, T0=T0: E.dma_start(
                out=xin[:, :, :],
                in_=x_loc[T0:T0 + TIL, :].rearrange("(s p) d -> p s d", p=128)), inc=dx, k=16)
            M.wait("vector", dx, dx.n)
            M.use("vector", "xbf")
            v = M.op("vector", lambda E: E.tensor_copy(out=xbf[:, :, :], in_=xin[:, :, :]), inc=sv)
            M.free("xin", sv, v)
            M.wait("tensor", sv, v)
            for sc in range(4):
                pt = PTs[sc % 2]
                M.use("tensor", f"PT{sc % 2}")
                for dc in range(8):
                    M.op("tensor", lambda E, sc=sc, dc=dc, pt=pt: E.transpose(
                        pt[:, dc * 128:dc * 128 + 128],
                        xbf[:, sc, dc * 128:dc * 128 + 128], ident_b[:, :]),
                        inc=(st if dc == 7 else None))
                M.wait("vector", st, st.n)
                v = M.op("vector", lambda E, sc=sc, pt=pt, T0=T0: E.tensor_copy(
                    out=xfm[:, :, T0 + sc * 128:T0 + sc * 128 + 128],
                    in_=pt[:, :].rearrange("p (a b) -> p a b", a=8)), inc=sv)
                M.free(f"PT{sc % 2}", sv, v)
            M.free("xbf", sv, sv.n)
            return sv.n

        def emit_proj_rope(tt, W_sb, dst_tile, spill_dst, dname, xrdy=None):
            T0 = tt * TIL
            M.wait("tensor", sv, sv.n if xrdy is None else xrdy)
            for h in range(H):
                pb = (h % 2) * 2
                M.use("tensor", f"P{pb}", f"P{pb + 1}")
                for kc2 in range(2):
                    kc = 2 * h + kc2
                    for dc in range(8):
                        M.op("tensor", lambda E, kc2=kc2, kc=kc, dc=dc, T0=T0, pb=pb: E.matmul(
                            P[pb + kc2][:, :], W_sb[:, dc, kc * 128:kc * 128 + 128],
                            xfm[:, dc, T0:T0 + TIL], start=(dc == 0), stop=(dc == 7)),
                            inc=(st if (dc == 7 and kc2 == 1) else None))
                M.wait("vector", st, st.n)
                if h == 0:
                    M.use("vector", dname, dname + "_do")
                cosb = cossin[:, 0, T0:T0 + TIL]
                sinb = cossin[:, 1, T0:T0 + TIL]
                M.op("vector", lambda E, cosb=cosb, pb=pb: E.tensor_tensor(out=rope1[:, :], in0=P[pb][:, :], in1=cosb, op=Op.mult))
                M.op("vector", lambda E, sinb=sinb, pb=pb: E.tensor_tensor(out=rope2[:, :], in0=P[pb + 1][:, :], in1=sinb, op=Op.mult))
                M.op("vector", lambda E, h=h, d=dst_tile: E.tensor_tensor(out=d[:, 2 * h, :], in0=rope1[:, :], in1=rope2[:, :], op=Op.subtract))
                M.op("vector", lambda E, cosb=cosb, pb=pb: E.tensor_tensor(out=rope1[:, :], in0=P[pb + 1][:, :], in1=cosb, op=Op.mult))
                M.op("vector", lambda E, sinb=sinb, pb=pb: E.tensor_tensor(out=rope2[:, :], in0=P[pb][:, :], in1=sinb, op=Op.mult))
                v = M.op("vector", lambda E, h=h, d=dst_tile: E.tensor_tensor(
                    out=d[:, 2 * h + 1, :], in0=rope1[:, :], in1=rope2[:, :], op=Op.add), inc=sv)
                M.free(f"P{pb}", sv, v)
                M.free(f"P{pb + 1}", sv, v)
            if spill_dst is not None:
                M.wait("sync", sv, sv.n)
                M.op("sync", lambda E, T0=T0, d=dst_tile: E.dma_start(
                    out=spill_dst[:, T0:T0 + TIL].rearrange("(a p) t -> p a t", p=128),
                    in_=d[:, :, :]), inc=do, k=16)
                M.free(dname + "_do", do, do.n)

        def emit_kcross(tt):
            T0 = tt * TIL
            M.use("vector", "kctile", "kctile_do")
            M.wait("tensor", sv, sv.n)
            for h in range(H):
                pt = PTs[h % 2]
                M.use("tensor", f"PT{h % 2}")
                for sc in range(4):
                    for kc2 in range(2):
                        M.op("tensor", lambda E, h=h, sc=sc, kc2=kc2, pt=pt: E.transpose(
                            pt[:, (sc * 2 + kc2) * 128:(sc * 2 + kc2) * 128 + 128],
                            ktile[:, 2 * h + kc2, sc * 128:sc * 128 + 128], ident_b[:, :]),
                            inc=(st if (sc == 3 and kc2 == 1) else None))
                M.wait("vector", st, st.n)
                for par in range(2):
                    v = M.op("vector", lambda E, h=h, par=par, pt=pt: E.tensor_scalar(
                        out=kctile[:, par::2, h, :],
                        in0=pt[:, :].rearrange("p (s c) -> p s c", s=4)[:, par::2, :],
                        scalar1=crossk[:, h, par:par + 1], scalar2=None, op0=Op.mult), inc=sv)
                M.free(f"PT{h % 2}", sv, v)
            M.wait("sync", sv, sv.n)
            for h in range(H):
                M.op("sync", lambda E, T0=T0, h=h: E.dma_start(
                    out=spill_kc[h, T0:T0 + TIL, :].rearrange("(s p) k -> p s k", p=128),
                    in_=kctile[:, :, h, :]), inc=do, k=16)
            M.free("kctile_do", do, do.n)

        def emit_vproj(tt):
            T0 = tt * TIL
            M.use("vector", "vtile", "vtile_do")
            for i, (sc, vc) in enumerate([(s, v) for s in range(4) for v in range(4)]):
                bname = f"P{2 + (i % 2)}"
                bank = P[2 + (i % 2)]
                M.use("tensor", bname)
                for dc in range(8):
                    M.op("tensor", lambda E, sc=sc, vc=vc, dc=dc, bank=bank, T0=T0: E.matmul(
                        bank[:, :], xfm[:, dc, T0 + sc * 128:T0 + sc * 128 + 128],
                        WvB[:, dc, vc * 512:vc * 512 + 512], start=(dc == 0), stop=(dc == 7)),
                        inc=(st if dc == 7 else None))
                M.wait("vector", st, st.n)
                v = M.op("vector", lambda E, sc=sc, vc=vc, bank=bank: E.tensor_copy(
                    out=vtile[:, sc, vc * 512:vc * 512 + 512], in_=bank[:, :]), inc=sv)
                M.free(bname, sv, v)
            M.wait("sync", sv, sv.n)
            M.op("sync", lambda E, T0=T0: E.dma_start(
                out=spill_v[T0:T0 + TIL, :].rearrange("(s p) d -> p s d", p=128),
                in_=vtile[:, :, :]), inc=do, k=16)
            M.free("vtile_do", do, do.n)

        def emit_retention(tt, with_o):
            M.wait("tensor", sv, sv.n)
            for cl in range(2):
                for h in range(H):
                    if with_o:
                        M.use("tensor", "P2")
                        for jc in range(2):
                            for kc2 in range(2):
                                M.op("tensor", lambda E, h=h, cl=cl, jc=jc, kc2=kc2: E.matmul(
                                    P[2][:, jc * 256:jc * 256 + 256],
                                    krt2[:, 2 * h + kc2, cl * 256 + jc * 128:cl * 256 + jc * 128 + 128],
                                    ktile[:, 2 * h + kc2, cl * 256:cl * 256 + 256],
                                    start=(kc2 == 0), stop=(kc2 == 1)),
                                    inc=(st if (jc == 1 and kc2 == 1) else None))
                        M.wait("vector", st, st.n)
                        M.use("vector", "attnT")
                        for jc in range(2):
                            v = M.op("vector", lambda E, h=h, jc=jc: E.tensor_tensor(
                                out=attnT[:, jc, :], in0=P[2][:, jc * 256:jc * 256 + 256],
                                in1=dmaskT[:, h, jc, :], op=Op.mult), inc=sv)
                        M.free("P2", sv, v)
                        M.wait("tensor", sv, v)
                        M.use("tensor", "P0", "P1", "P2", "P3", "Sb_w")
                        for ic in range(2):
                            bi = P[0] if ic == 0 else P[2]
                            bI = P[1] if ic == 0 else P[3]
                            for jc in range(2):
                                M.op("tensor", lambda E, h=h, cl=cl, ic=ic, jc=jc, bi=bi: E.matmul(
                                    bi[:, :], attnT[:, jc, ic * 128:ic * 128 + 128],
                                    vtile[:, cl * 2 + jc, h * 512:h * 512 + 512],
                                    start=(jc == 0), stop=(jc == 1)))
                            for kc2 in range(2):
                                M.op("tensor", lambda E, h=h, cl=cl, ic=ic, kc2=kc2, bI=bI: E.matmul(
                                    bI[:, :], ktile[:, 2 * h + kc2, cl * 256 + ic * 128:cl * 256 + ic * 128 + 128],
                                    Sb[:, h, kc2, :], start=(kc2 == 0), stop=(kc2 == 1)),
                                    inc=(st if (ic == 1 and kc2 == 1) else None))
                        ST_O = st.n
                        M.free("attnT", st, ST_O)
                    M.use("tensor", "P4", "P5")
                    for kc2 in range(2):
                        for jc in range(2):
                            M.op("tensor", lambda E, h=h, cl=cl, kc2=kc2, jc=jc: E.matmul(
                                P[4 + kc2][:, :], kctile[:, cl * 2 + jc, h, kc2 * 128:kc2 * 128 + 128],
                                vtile[:, cl * 2 + jc, h * 512:h * 512 + 512],
                                start=(jc == 0), stop=(jc == 1)),
                                inc=(st if (kc2 == 1 and jc == 1) else None))
                    ST_AC = st.n
                    if with_o:
                        M.wait("vector", st, ST_O)
                        for ic in range(2):
                            bi = P[0] if ic == 0 else P[2]
                            bI = P[1] if ic == 0 else P[3]
                            sct = cl * 2 + ic
                            M.use("vector", "obuf")
                            M.op("vector", lambda E, h=h, ic=ic, bI=bI: E.tensor_scalar(
                                out=obuf[:, :], in0=bI[:, :], scalar1=crossq[:, h, ic:ic + 1],
                                scalar2=None, op0=Op.mult))
                            v = M.op("vector", lambda E, bi=bi: E.tensor_tensor(
                                out=obuf[:, :], in0=obuf[:, :], in1=bi[:, :], op=Op.add), inc=sv)
                            M.free("P0" if ic == 0 else "P2", sv, v)
                            M.free("P1" if ic == 0 else "P3", sv, v)
                            M.wait("scalar", sv, v)
                            vsq = M.op("scalar", lambda E: E.square(osq[:, :], obuf[:, :]), inc=ss)
                            M.free("obuf", ss, vsq)
                            M.wait("vector", ss, vsq)
                            M.op("vector", lambda E: E.tensor_reduce(
                                out=osc[:, 0:1], in_=osq[:, :], axis=mybir.AxisListType.X, op=Op.add))
                            M.op("vector", lambda E: E.drain())
                            v = M.op("vector", lambda E: E.tensor_scalar(
                                out=osc[:, 1:2], in0=osc[:, 0:1], scalar1=1.0 / DV, scalar2=EPS,
                                op0=Op.mult, op1=Op.add), inc=sv)
                            M.wait("scalar", sv, v)
                            v = M.op("scalar", lambda E: E.sqrt(osc[:, 2:3], osc[:, 1:2]), inc=ss)
                            M.wait("vector", ss, v)
                            M.op("vector", lambda E: E.reciprocal(osc[:, 3:4], osc[:, 2:3]))
                            M.op("vector", lambda E: E.drain())
                            M.op("vector", lambda E, h=h, sct=sct: E.scalar_tensor_tensor(
                                out=ogat[:, sct, h * 512:h * 512 + 512], in0=obuf[:, :],
                                scalar=osc[:, 3:4], in1=gn_b[:, :], op0=Op.mult, op1=Op.mult), inc=sv)
                    M.wait("vector", st, ST_AC)
                    for kc2 in range(2):
                        M.op("vector", lambda E, h=h, kc2=kc2, g=GC[h]: E.scalar_tensor_tensor(
                            out=Sm[:, h, kc2, :], in0=Sm[:, h, kc2, :], scalar=g,
                            in1=P[4 + kc2][:, :], op0=Op.mult, op1=Op.add))
                    v = M.op("vector", lambda E, h=h: E.tensor_copy(
                        out=Sb[:, h, :, :].rearrange("p a b -> p (a b)"),
                        in_=Sm[:, h, :, :].rearrange("p a b -> p (a b)")), inc=sv)
                    M.free("P4", sv, v)
                    M.free("P5", sv, v)
                    M.free("Sb_w", sv, v)

        M.wait("tensor", dw, DW_P1)
        M.wait("tensor", sv, SM_ZERO)
        for tt in range(NTIL):
            emit_xfm_tile(tt)
            emit_proj_rope(tt, WkA, ktile, spill_kr, "ktile")
            emit_kcross(tt)
            emit_vproj(tt)
            emit_retention(tt, with_o=False)
            M.free("ktile", st, st.n)
            M.free("vtile", st, st.n)
            M.free("kctile", st, st.n)
        M.wait("sync", sv, sv.n)
        M.op("sync", lambda E: E.dma_start(
            out=ag_s_in[:, :].rearrange("(a p) v -> p a v", p=128),
            in_=Sm[:, :, :, :].rearrange("p h k v -> p (h k) v")), inc=do, k=16)
        if "p1dump" in dbg:
            d_sm = dout("dbg_sm", [128, H * 2 * DV])
            d_kt = dout("dbg_ktile", [128, 8 * TIL], BF16)
            d_vt = dout("dbg_vtile", [128, 4 * VDIM], BF16)
            d_kc = dout("dbg_kctile", [128, 4 * H * 256], BF16)
            d_xf = dout("dbg_xfm", [128, 8 * NT], BF16)
            M.wait("sync", sv, sv.n)
            M.op("sync", lambda E: E.dma_start(out=d_sm[:, :], in_=Sm[:, :, :, :].rearrange("p a b c -> p (a b c)")), inc=do, k=16)
            M.op("sync", lambda E: E.dma_start(out=d_kt[:, :], in_=ktile[:, :, :].rearrange("p a b -> p (a b)")), inc=do, k=16)
            M.op("sync", lambda E: E.dma_start(out=d_vt[:, :], in_=vtile[:, :, :].rearrange("p a b -> p (a b)")), inc=do, k=16)
            M.op("sync", lambda E: E.dma_start(out=d_kc[:, :], in_=kctile[:, :, :, :].rearrange("p a b c -> p (a b c)")), inc=do, k=16)
            M.op("sync", lambda E: E.dma_start(out=d_xf[:, :], in_=xfm[:, :, :].rearrange("p a b -> p (a b)")), inc=do, k=16)
            d_cs = dout("dbg_cossin", [128, 2 * NT], F16)
            M.op("sync", lambda E: E.dma_start(out=d_cs[:, :], in_=cossin[:, :, :].rearrange("p a b -> p (a b)")), inc=do, k=16)
        M.use("gpsimd", "ktile", "ktile_do")
        M.op("gpsimd", lambda E: E.dma_start(
            out=WkA[:, :, :],
            in_=Wq_i[:, :].rearrange("(a p) b -> p a b", p=128)), inc=dw, k=16)
        M.wait("gpsimd", do, do.n)
        M.op("gpsimd", lambda E: E.collective_compute(
            "AllGather", Op.bypass, replica_groups=[list(range(NCORES))],
            ins=[ag_s_in.opt()], outs=[ag_s_out.opt()]), inc=cc)

        def emit_s_gather():
            M.wait("gpsimd", cc, cc.n)
            for blk in range(8):
                h, kc2 = blk // 2, blk % 2
                M.op("gpsimd", lambda E, h=h, kc2=kc2, blk=blk: E.indirect_dma_start(
                    out=Sm[:, h, kc2, :], out_offset=None, in_=ag_s_out[:, :],
                    in_offset=IndirectOffsetOnAxis(ap=sinitr[:, blk:blk + 1], axis=0)), inc=dg, k=16)
            M.wait("vector", dg, dg.n)
            M.op("vector", lambda E: E.tensor_scalar(
                out=Sm[:, :, :, :].rearrange("p a b c -> p (a b c)"),
                in0=Sm[:, :, :, :].rearrange("p a b c -> p (a b c)"),
                scalar1=gate_s[:, 0:1], scalar2=None, op0=Op.mult))
            M.op("vector", lambda E: E.tensor_copy(
                out=Sb[:, :, :, :].rearrange("p a b c -> p (a b c)"),
                in_=Sm[:, :, :, :].rearrange("p a b c -> p (a b c)")), inc=sv)
            M.free("Sb_w", sv, sv.n)
        if "s8" in dbg:
            sdbg = dout("dbg_sinit", [128, H * 2 * DV])
            M.wait("sync", sv, sv.n)
            M.op("sync", lambda E, d=sdbg: E.dma_start(
                out=d[:, :], in_=Sm[:, :, :, :].rearrange("p a b c -> p (a b c)")), inc=do, k=16)

        if on("p2"):
            M.wait("tensor", dw, dw.n)
            DO_SPILL1 = do.n
            for tt in range(NTIL):
                T0 = tt * TIL
                M.wait("sync", do, DO_SPILL1)
                M.use("sync", "krt2", "vtile", "kctile")
                M.op("sync", lambda E, T0=T0: E.dma_start(
                    out=krt2[:, :, :],
                    in_=spill_kr[:, T0:T0 + TIL].rearrange("(a p) t -> p a t", p=128)), inc=dx, k=16)
                M.op("sync", lambda E, T0=T0: E.dma_start(
                    out=vtile[:, :, :],
                    in_=spill_v[T0:T0 + TIL, :].rearrange("(s p) d -> p s d", p=128)), inc=dx, k=16)
                for h in range(H):
                    M.op("sync", lambda E, T0=T0, h=h: E.dma_start(
                        out=kctile[:, :, h, :],
                        in_=spill_kc[h, T0:T0 + TIL, :].rearrange("(s p) k -> p s k", p=128)), inc=dx, k=16)
                DX_RELOAD = dx.n
                emit_proj_rope(tt, WkA, ktile, None, "ktile")
                if tt == 0:
                    emit_s_gather()
                M.wait("tensor", dx, DX_RELOAD)
                M.use("vector", "ogat")
                emit_retention(tt, with_o=True)
                M.wait("sync", sv, sv.n)
                M.op("sync", lambda E, T0=T0: E.dma_start(
                    out=spill_on[T0:T0 + TIL, :].rearrange("(s p) d -> p s d", p=128),
                    in_=ogat[:, :, :]), inc=do, k=16)
                M.free("ogat", do, do.n)
                M.free("ktile", st, st.n)
                M.free("krt2", st, st.n)
                M.free("vtile", st, st.n)
                M.free("kctile", st, st.n)

        if "p2dump" in dbg:
            d_at = dout("dbg_attnT", [128, 2 * CH], BF16)
            d_qt = dout("dbg_qrot", [128, 8 * TIL], BF16)
            M.wait("sync", sv, sv.n)
            M.op("sync", lambda E: E.dma_start(out=d_at[:, :], in_=attnT[:, :, :].rearrange("p a b -> p (a b)")), inc=do, k=16)
            M.op("sync", lambda E: E.dma_start(out=d_qt[:, :], in_=ktile[:, :, :].rearrange("p a b -> p (a b)")), inc=do, k=16)
            d_ob = dout("dbg_obuf", [128, DV])
            M.op("sync", lambda E: E.dma_start(out=d_ob[:, :], in_=obuf[:, :]), inc=do, k=16)
            d_osc = dout("dbg_osc", [128, 16])
            d_osq = dout("dbg_osq", [128, DV])
            M.op("sync", lambda E: E.dma_start(out=d_osc[:, :], in_=osc[:, :]), inc=do, k=16)
            M.op("sync", lambda E: E.dma_start(out=d_osq[:, :], in_=osq[:, :]), inc=do, k=16)
            d_on = dout("dbg_onorm", [NT, VDIM], BF16)
            M.wait("sync", do, do.n)
            for tt in range(NTIL):
                T0 = tt * TIL
                M.op("sync", lambda E, T0=T0: E.dma_start(
                    out=ogat[:, :, :],
                    in_=spill_on[T0:T0 + TIL, :].rearrange("(s p) d -> p s d", p=128)), inc=dx, k=16)
                M.wait("sync", dx, dx.n)
                M.op("sync", lambda E, T0=T0, d=d_on: E.dma_start(
                    out=d[T0:T0 + TIL, :].rearrange("(s p) d -> p s d", p=128),
                    in_=ogat[:, :, :]), inc=do, k=16)
                M.wait("sync", do, do.n)

        if on("a2b"):
            M.wait("gpsimd", st, st.n)
            M.wait("gpsimd", sv, sv.n)
            M.wait("gpsimd", do, do.n)
            M.wait("gpsimd", dx, dx.n)
            M.op("gpsimd", lambda E: E.dma_start(
                out=WgB[:, :, :],
                in_=Wg_i[:, :].rearrange("(a p) b -> p a b", p=128)), inc=dw, k=16)
            M.op("gpsimd", lambda E: E.dma_start(
                out=WoC[:, :, :],
                in_=Wo_i[:, :].rearrange("(a p) b -> p a b", p=128)), inc=dw, k=16)
            DW_A2B = dw.n
            M.wait("tensor", dw, DW_A2B)
            DO_ON = do.n
            if "t0dump" in dbg:
                d_pre0 = dout("dbg_pre0", [128, 1024])
                d_osc0 = dout("dbg_osc0", [128, 16])
            for tt in range(NTIL):
                T0 = tt * TIL
                M.wait("sync", do, DO_ON)
                M.use("sync", "ogat_b", "xin")
                M.op("sync", lambda E, T0=T0: E.dma_start(
                    out=ogat_b[:, :, :],
                    in_=spill_on[T0:T0 + TIL, :].rearrange("(s p) d -> p s d", p=128)), inc=dx, k=16)
                M.op("sync", lambda E, T0=T0: E.dma_start(
                    out=xin[:, :, :],
                    in_=x_loc[T0:T0 + TIL, :].rearrange("(s p) d -> p s d", p=128)), inc=dx, k=16)
                DX_T = dx.n
                M.use("scalar", "ogat2_w")
                for i, (sc, vc) in enumerate([(s, v) for s in range(4) for v in range(4)]):
                    bname = f"P{i % 4}"
                    bank = P[i % 4]
                    M.use("tensor", bname)
                    for dc in range(8):
                        M.op("tensor", lambda E, sc=sc, vc=vc, dc=dc, bank=bank, T0=T0: E.matmul(
                            bank[:, :], xfm[:, dc, T0 + sc * 128:T0 + sc * 128 + 128],
                            WgB[:, dc, vc * 512:vc * 512 + 512], start=(dc == 0), stop=(dc == 7)),
                            inc=(st if dc == 7 else None))
                    M.wait("scalar", st, st.n)
                    vs = M.op("scalar", lambda E, sc=sc, vc=vc, bank=bank: E.activation(
                        out=ogat2[:, sc, vc * 512:vc * 512 + 512], in_=bank[:, :], func=AF.Silu), inc=ss)
                    M.free(bname, ss, vs)
                    M.wait("vector", ss, vs)
                    if i == 0:
                        M.wait("vector", dx, DX_T)
                    M.op("vector", lambda E, sc=sc, vc=vc: E.tensor_tensor(
                        out=ogat2[:, sc, vc * 512:vc * 512 + 512],
                        in0=ogat2[:, sc, vc * 512:vc * 512 + 512],
                        in1=ogat_b[:, sc, vc * 512:vc * 512 + 512], op=Op.mult),
                        inc=(sv if i == 15 else None))
                GAT_DONE = sv.n
                M.free("ogat_b", sv, GAT_DONE)
                if A2B_SUB < 2:
                    continue
                for sc in range(4):
                    M.wait("tensor", sv, GAT_DONE)
                    for half in range(2):
                        pt = PTs[(2 * sc + half) % 2]
                        M.use("tensor", f"PT{(2 * sc + half) % 2}")
                        for fcl in range(8):
                            fc = half * 8 + fcl
                            M.op("tensor", lambda E, sc=sc, fc=fc, fcl=fcl, pt=pt: E.transpose(
                                pt[:, fcl * 128:fcl * 128 + 128],
                                ogat2[:, sc, fc * 128:fc * 128 + 128], ident_b[:, :]),
                                inc=(st if fcl == 7 else None))
                        M.wait("vector", st, st.n)
                        v = M.op("vector", lambda E, half=half, sc=sc, pt=pt: E.tensor_copy(
                            out=ogfm[:, half * 8:half * 8 + 8, sc * 128:sc * 128 + 128],
                            in_=pt[:, :].rearrange("p (a b) -> p a b", a=8)), inc=sv)
                        M.free(f"PT{(2 * sc + half) % 2}", sv, v)
                    OG_RDY = sv.n
                    M.wait("tensor", sv, OG_RDY)
                    M.use("tensor", "P4", "P5")
                    for dmc in range(2):
                        for fc in range(16):
                            M.op("tensor", lambda E, sc=sc, dmc=dmc, fc=fc: E.matmul(
                                P[4 + dmc][:, :], ogfm[:, fc, sc * 128:sc * 128 + 128],
                                WoC[:, fc, dmc * 512:dmc * 512 + 512],
                                start=(fc == 0), stop=(fc == 15)),
                                inc=(st if (fc == 15 and dmc == 1) else None))
                    M.wait("vector", st, st.n)
                    if A2B_SUB < 3:
                        continue
                    for dmc in range(2):
                        M.op("vector", lambda E, sc=sc, dmc=dmc: E.tensor_tensor(
                            out=x1pre[:, dmc * 512:dmc * 512 + 512], in0=P[4 + dmc][:, :],
                            in1=xin[:, sc, dmc * 512:dmc * 512 + 512], op=Op.add),
                            inc=(sv if dmc == 1 else None))
                    M.free("P4", sv, sv.n)
                    M.free("P5", sv, sv.n)
                    if "t0dump" in dbg and tt == 0 and sc == 1:
                        M.wait("sync", sv, sv.n)
                        M.op("sync", lambda E: E.dma_start(out=d_pre0[:, :], in_=x1pre[:, :]), inc=do, k=16)
                    vpre = M.op("vector", lambda E: E.tensor_reduce(
                        out=osc[:, 4:5], in_=x1pre[:, :], axis=mybir.AxisListType.X, op=Op.add), inc=sv)
                    M.wait("scalar", sv, vpre)
                    vsq = M.op("scalar", lambda E: E.square(osqb[:, :], x1pre[:, :]), inc=ss)
                    M.wait("vector", ss, vsq)
                    M.op("vector", lambda E: E.tensor_reduce(
                        out=osc[:, 5:6], in_=osqb[:, :], axis=mybir.AxisListType.X, op=Op.add))
                    M.op("vector", lambda E: E.drain())
                    M.op("vector", lambda E: E.tensor_scalar(
                        out=osc[:, 6:7], in0=osc[:, 4:5], scalar1=-1.0 / D_, scalar2=None, op0=Op.mult))
                    M.op("vector", lambda E: E.tensor_scalar(
                        out=osc[:, 7:8], in0=osc[:, 5:6], scalar1=1.0 / D_, scalar2=EPS, op0=Op.mult, op1=Op.add))
                    M.op("vector", lambda E: E.tensor_tensor(
                        out=osc[:, 8:9], in0=osc[:, 6:7], in1=osc[:, 6:7], op=Op.mult))
                    M.op("vector", lambda E: E.drain())
                    v = M.op("vector", lambda E: E.tensor_tensor(
                        out=osc[:, 10:11], in0=osc[:, 7:8], in1=osc[:, 8:9], op=Op.subtract), inc=sv)
                    M.op("vector", lambda E: E.drain())
                    M.wait("scalar", sv, v)
                    v = M.op("scalar", lambda E: E.sqrt(osc[:, 11:12], osc[:, 10:11]), inc=ss)
                    M.wait("vector", ss, v)
                    M.op("vector", lambda E: E.reciprocal(osc[:, 12:13], osc[:, 11:12]))
                    M.op("vector", lambda E: E.drain())
                    if "t0dump" in dbg and tt == 0 and sc == 1:
                        vz = M.op("vector", lambda E: E.tensor_copy(out=osc[:, 14:15], in_=osc[:, 12:13]), inc=sv)
                        M.wait("sync", sv, vz)
                        M.op("sync", lambda E: E.dma_start(out=d_osc0[:, :], in_=osc[:, :]), inc=do, k=16)
                        M.wait("vector", do, do.n)
                    M.op("vector", lambda E: E.tensor_scalar(
                        out=x1pre[:, :], in0=x1pre[:, :], scalar1=osc[:, 6:7], scalar2=osc[:, 12:13],
                        op0=Op.add, op1=Op.mult))
                    M.use("vector", "x1btile_d", "x1btile_t")
                    X1_RDY = M.op("vector", lambda E: E.tensor_copy(out=x1btile[:, :], in_=x1pre[:, :]), inc=sv)
                    M.wait("sync", sv, X1_RDY)
                    M.op("sync", lambda E, T0=T0, sc=sc: E.dma_start(
                        out=x1f_d[T0 + sc * 128:T0 + sc * 128 + 128, :], in_=x1pre[:, :]), inc=do, k=16)
                    M.free("x1pre_d", do, do.n)
                    M.op("sync", lambda E, T0=T0, sc=sc: E.dma_start(
                        out=x1b_in[T0 + sc * 128:T0 + sc * 128 + 128, :], in_=x1btile[:, :]), inc=do, k=16)
                    M.free("x1btile_d", do, do.n)
                    pt2 = PTs[(sc + 1) % 2]
                    M.use("tensor", f"PT{(sc + 1) % 2}", "x1fmt_r")
                    M.wait("tensor", sv, X1_RDY)
                    for dc in range(8):
                        M.op("tensor", lambda E, dc=dc, pt2=pt2: E.transpose(
                            pt2[:, dc * 128:dc * 128 + 128], x1btile[:, dc * 128:dc * 128 + 128],
                            ident_b[:, :]), inc=(st if dc == 7 else None))
                    M.free("x1btile_t", st, st.n)
                    M.wait("vector", st, st.n)
                    v = M.op("vector", lambda E, sc=sc, pt2=pt2: E.tensor_copy(
                        out=x1fmt[:, :, sc * 128:sc * 128 + 128],
                        in_=pt2[:, :].rearrange("p (a b) -> p a b", a=8)), inc=sv)
                    M.free(f"PT{(sc + 1) % 2}", sv, v)
                M.free("xin", sv, sv.n)
                M.free("ogat2_w", st, st.n)
                if A2B_SUB < 4:
                    continue
                M.wait("tensor", sv, sv.n)
                M.use("tensor", "P0")
                for dc in range(8):
                    M.op("tensor", lambda E, dc=dc: E.matmul(
                        P[0][0:16, :], wgate[:, dc, :], x1fmt[:, dc, :],
                        start=(dc == 0), stop=(dc == 7)), inc=(st if dc == 7 else None))
                M.free("x1fmt_r", st, st.n)
                M.wait("scalar", st, st.n)
                M.use("scalar", "obuf_l")
                v = M.op("scalar", lambda E: E.activation(out=explg[0:16, :], in_=P[0][0:16, :], func=AF.Exp), inc=ss)
                M.free("P0", ss, v)
                M.wait("tensor", ss, v)
                M.use("tensor", "P5")
                v = M.op("tensor", lambda E: E.matmul(
                    P[5][0:1, :], ones16[:, :], explg[0:16, :], start=True, stop=True), inc=st)
                M.wait("vector", st, v)
                v = M.op("vector", lambda E: E.tensor_copy(out=explg[32:33, :], in_=P[5][0:1, :]), inc=sv)
                M.free("P5", sv, v)
                M.wait("sync", sv, v)
                M.wait("sync", ss, ss.n)
                M.op("sync", lambda E, T0=T0: E.dma_start(
                    out=explog_in[:, T0:T0 + TIL], in_=explg[0:16, :]), inc=do, k=16)
                M.op("sync", lambda E, T0=T0: E.dma_start(
                    out=den_in[:, T0:T0 + TIL], in_=explg[32:33, :]), inc=do, k=16)
                M.free("obuf_l", do, do.n)
            if "x1out" in dbg:
                d_on0 = dout("dbg_on0", [TIL, VDIM], BF16)
                M.wait("sync", do, do.n)
                M.op("sync", lambda E: E.dma_start(
                    out=ogat2[:, :, :],
                    in_=spill_on[0:TIL, :].rearrange("(s p) d -> p s d", p=128)), inc=dx, k=16)
                M.wait("sync", dx, dx.n)
                M.op("sync", lambda E, d=d_on0: E.dma_start(
                    out=d[0:TIL, :].rearrange("(s p) d -> p s d", p=128),
                    in_=ogat2[:, :, :]), inc=do, k=16)
                d_og2 = dout("dbg_ogat2", [128, 4 * VDIM], BF16)
                d_ogb = dout("dbg_ogatb", [128, 4 * VDIM], BF16)
                M.op("sync", lambda E, d=d_ogb: E.dma_start(out=d[:, :], in_=ogat_b[:, :, :].rearrange("p a b -> p (a b)")), inc=do, k=16)
                d_wgb = dout("dbg_wgb", [128, 8 * 2048], BF16)
                M.op("sync", lambda E, d=d_wgb: E.dma_start(out=d[:, :], in_=WgB[:, :, :].rearrange("p a b -> p (a b)")), inc=do, k=16)
                d_ogfm = dout("dbg_ogfm", [128, 16 * TIL], BF16)
                d_x1p = dout("dbg_x1pre", [128, 1024])
                M.wait("sync", sv, sv.n)
                M.op("sync", lambda E, d=d_og2: E.dma_start(out=d[:, :], in_=ogat2[:, :, :].rearrange("p a b -> p (a b)")), inc=do, k=16)
                M.op("sync", lambda E, d=d_ogfm: E.dma_start(out=d[:, :], in_=ogfm[:, :, :].rearrange("p a b -> p (a b)")), inc=do, k=16)
                M.op("sync", lambda E, d=d_x1p: E.dma_start(out=d[:, :], in_=x1pre[:, :]), inc=do, k=16)
                x1dbg = dout("dbg_x1", [NT, D_])
                M.wait("sync", do, do.n)
                M.op("sync", lambda E, d=x1dbg: E.dma_start(out=d[:, :], in_=x1f_d[:, :]), inc=do, k=16)

        if on("broute"):
            M.wait("gpsimd", st, st.n)
            M.wait("gpsimd", sv, sv.n)
            M.wait("gpsimd", do, do.n)
            M.wait("gpsimd", dx, dx.n)
            M.op("gpsimd", lambda E: E.dma_start(
                out=WsgT[:, :, :],
                in_=WsgT_i[:, :].rearrange("(a p) b -> p a b", p=128)), inc=dw, k=16)
            M.op("gpsimd", lambda E: E.dma_start(
                out=WsuT[:, :, :],
                in_=WsuT_i[:, :].rearrange("(a p) b -> p a b", p=128)), inc=dw, k=16)
            M.op("gpsimd", lambda E: E.dma_start(
                out=WsdT[:, :, :],
                in_=WsdT_i[:, :].rearrange("(a p) b -> p a b", p=128)), inc=dw, k=16)
            DW_B = dw.n
            M.wait("gpsimd", do, do.n)
            M.op("gpsimd", lambda E: E.collective_compute(
                "AllGather", Op.bypass, replica_groups=[list(range(NCORES))],
                ins=[x1b_in.opt()], outs=[x1all.opt()]), inc=cc)
            M.op("gpsimd", lambda E: E.collective_compute(
                "AllGather", Op.bypass, replica_groups=[list(range(NCORES))],
                ins=[explog_in.opt()], outs=[explog_out.opt()]), inc=cc)
            M.op("gpsimd", lambda E: E.collective_compute(
                "AllGather", Op.bypass, replica_groups=[list(range(NCORES))],
                ins=[den_in.opt()], outs=[den_out.opt()]), inc=cc)
            CC_AG1 = cc.n
            M.wait("tensor", dw, DW_B)
            M.wait("sync", st, st.n)
            PRELUDE_DONE = [False]

            def emit_route_prelude():
                M.wait("gpsimd", cc, CC_AG1)
                for hh in range(2):
                    M.op("gpsimd", lambda E, hh=hh: E.indirect_dma_start(
                        out=scoresR[:, hh * 2048:hh * 2048 + 2048], out_offset=None,
                        in_=explog_out[:, :],
                        in_offset=IndirectOffsetOnAxis(ap=elr_s[:, hh:hh + 1], axis=0)), inc=dg, k=16)
                    M.op("gpsimd", lambda E, hh=hh: E.indirect_dma_start(
                        out=denT[:, hh * 2048:hh * 2048 + 2048], out_offset=None,
                        in_=den_out[:, :],
                        in_offset=IndirectOffsetOnAxis(ap=denr_s[:, hh:hh + 1], axis=0)), inc=dg, k=16)
                DG_SC = dg.n
                M.wait("vector", st, st.n)
                M.wait("vector", dg, DG_SC)
                M.op("vector", lambda E: E.reciprocal(denT[:, :], denT[:, :]))
                M.op("vector", lambda E: E.drain())
                SCORES_RDY = M.op("vector", lambda E: E.tensor_tensor(
                    out=scoresR[:, :], in0=scoresR[:, :], in1=denT[:, :], op=Op.mult), inc=sv)
                M.wait("sync", sv, SCORES_RDY)
                M.op("sync", lambda E: E.dma_start(
                    out=scores_d[:, :].rearrange("(a b) o -> a (b o)", a=8), in_=scoresR[:, :]), inc=do, k=16)
                M.op("vector", lambda E: E.memset(sc8[:, 0:1], 0.0))
                M.op("vector", lambda E: E.memset(sc8[:, 1:2], 1.0), inc=sv)
                PRELUDE_DONE[0] = True
                return do.n

            def emit_bis_iter():
                M.op("vector", lambda E: E.tensor_scalar(
                    out=sc8[:, 2:3], in0=sc8[:, 0:1], scalar1=sc8[:, 1:2], scalar2=0.5,
                    op0=Op.add, op1=Op.mult))
                M.op("vector", lambda E: E.drain())
                M.op("vector", lambda E: E.tensor_scalar(
                    out=mdump[:, :], in0=scoresR[:, :], scalar1=sc8[:, 2:3], scalar2=0.0,
                    op0=Op.is_ge, op1=Op.add, accum_out=sc8[:, 3:4]))
                M.op("vector", lambda E: E.drain())
                M.op("vector", lambda E: E.tensor_scalar(
                    out=sc8[:, 4:5], in0=sc8[:, 3:4], scalar1=float(CAP), scalar2=None, op0=Op.is_ge))
                M.op("vector", lambda E: E.tensor_tensor(
                    out=sc8[:, 6:7], in0=sc8[:, 2:3], in1=sc8[:, 0:1], op=Op.subtract))
                M.op("vector", lambda E: E.tensor_tensor(
                    out=sc8[:, 7:8], in0=sc8[:, 2:3], in1=sc8[:, 1:2], op=Op.subtract))
                M.op("vector", lambda E: E.tensor_scalar(
                    out=sc8[:, 5:6], in0=sc8[:, 4:5], scalar1=-1.0, scalar2=1.0, op0=Op.mult, op1=Op.add))
                M.op("vector", lambda E: E.scalar_tensor_tensor(
                    out=sc8[:, 0:1], in0=sc8[:, 6:7], scalar=sc8[:, 4:5], in1=sc8[:, 0:1],
                    op0=Op.mult, op1=Op.add))
                M.op("vector", lambda E: E.drain())
                M.op("vector", lambda E: E.scalar_tensor_tensor(
                    out=sc8[:, 1:2], in0=sc8[:, 7:8], scalar=sc8[:, 5:6], in1=sc8[:, 1:2],
                    op0=Op.mult, op1=Op.add), inc=sv)
                M.op("vector", lambda E: E.drain())

            def emit_b_tile(bt):
                T0 = bt * 256
                M.use("sync", "x1btm")
                M.op("sync", lambda E, T0=T0: E.dma_start(
                    out=x1btm[:, :, :],
                    in_=x1b_in[T0:T0 + 256, :].rearrange("(s p) d -> p s d", p=128)), inc=dx, k=16)
                M.wait("tensor", dx, dx.n)
                M.use("tensor", "x1fmb_r")
                for scb in range(2):
                    pt = PTs[scb]
                    M.use("tensor", f"PT{scb}")
                    for dc in range(8):
                        M.op("tensor", lambda E, scb=scb, dc=dc, pt=pt: E.transpose(
                            pt[:, dc * 128:dc * 128 + 128],
                            x1btm[:, scb, dc * 128:dc * 128 + 128], ident_b[:, :]),
                            inc=(st if dc == 7 else None))
                    M.wait("vector", st, st.n)
                    v = M.op("vector", lambda E, scb=scb, pt=pt: E.tensor_copy(
                        out=x1fmb[:, :, scb * 128:scb * 128 + 128],
                        in_=pt[:, :].rearrange("p (a b) -> p a b", a=8)), inc=sv)
                    M.free(f"PT{scb}", sv, v)
                FMB_RDY = sv.n
                M.free("x1btm", sv, FMB_RDY)
                M.wait("tensor", sv, FMB_RDY)
                M.use("scalar", "hsb_w")
                for fc in range(16):
                    pb = (fc % 2) * 2
                    M.use("tensor", f"P{pb}", f"P{pb + 1}")
                    for dc in range(8):
                        M.op("tensor", lambda E, fc=fc, dc=dc, pb=pb: E.matmul(
                            P[pb][:, 0:256], WsgT[:, dc, fc * 128:fc * 128 + 128],
                            x1fmb[:, dc, :], start=(dc == 0), stop=(dc == 7)))
                        M.op("tensor", lambda E, fc=fc, dc=dc, pb=pb: E.matmul(
                            P[pb + 1][:, 0:256], WsuT[:, dc, fc * 128:fc * 128 + 128],
                            x1fmb[:, dc, :], start=(dc == 0), stop=(dc == 7)),
                            inc=(st if dc == 7 else None))
                    M.wait("scalar", st, st.n)
                    vs = M.op("scalar", lambda E, fc=fc, pb=pb: E.activation(
                        out=hsb[:, fc, :], in_=P[pb][:, 0:256], func=AF.Gelu), inc=ss)
                    M.free(f"P{pb}", ss, vs)
                    M.wait("vector", ss, vs)
                    v = M.op("vector", lambda E, fc=fc, pb=pb: E.tensor_tensor(
                        out=hsb[:, fc, :], in0=hsb[:, fc, :], in1=P[pb + 1][:, 0:256], op=Op.mult), inc=sv)
                    M.free(f"P{pb + 1}", sv, v)
                HS_RDY = sv.n
                M.free("x1fmb_r", st, st.n)
                M.wait("tensor", sv, HS_RDY)
                M.use("sync", "x1frel")
                M.op("sync", lambda E, T0=T0: E.dma_start(
                    out=x1frel[:, :, :],
                    in_=x1f_d[T0:T0 + 256, :].rearrange("(s p) d -> p s d", p=128)), inc=dx, k=16)
                DX_X1F = dx.n
                for tcs in range(2):
                    for dmc in range(2):
                        bname = f"P{2 + tcs * 2 + dmc}"
                        bank = P[2 + tcs * 2 + dmc]
                        M.use("tensor", bname)
                        for fc in range(16):
                            M.op("tensor", lambda E, tcs=tcs, dmc=dmc, fc=fc, bank=bank: E.matmul(
                                bank[:, :], hsb[:, fc, tcs * 128:tcs * 128 + 128],
                                WsdT[:, fc, dmc * 512:dmc * 512 + 512],
                                start=(fc == 0), stop=(fc == 15)),
                                inc=(st if fc == 15 else None))
                ST_SH = st.n
                M.free("hsb_w", st, ST_SH)
                M.wait("vector", st, ST_SH)
                M.wait("vector", dx, DX_X1F)
                for tcs in range(2):
                    for dmc in range(2):
                        bank = P[2 + tcs * 2 + dmc]
                        M.op("vector", lambda E, tcs=tcs, dmc=dmc, bank=bank: E.tensor_tensor(
                            out=x1frel[:, tcs, dmc * 512:dmc * 512 + 512],
                            in0=x1frel[:, tcs, dmc * 512:dmc * 512 + 512],
                            in1=bank[:, :], op=Op.add),
                            inc=(sv if (tcs == 1 and dmc == 1) else None))
                COMB_RDY = sv.n
                for i in range(4):
                    M.free(f"P{2 + i}", sv, COMB_RDY)
                M.wait("gpsimd", sv, COMB_RDY)
                for tcs in range(2):
                    M.op("gpsimd", lambda E, bt=bt, tcs=tcs: E.indirect_dma_start(
                        out=acc_d[:, :],
                        out_offset=IndirectOffsetOnAxis(ap=accr_s[:, bt * 2 + tcs:bt * 2 + tcs + 1], axis=0),
                        in_=x1frel[:, tcs, :], in_offset=None), inc=dg, k=16)
                M.free("x1frel", dg, dg.n)

            bis_done = 0
            DO_SCORES = None
            for bt in range(8):
                emit_b_tile(bt)
                if bt == 3:
                    DO_SCORES = emit_route_prelude()
                if bt >= 4:
                    for _ in range(8):
                        if bis_done < BIS_IT:
                            emit_bis_iter()
                            bis_done += 1
            while bis_done < BIS_IT:
                emit_bis_iter()
                bis_done += 1
            DG_ACC0 = dg.n
            M.op("vector", lambda E: E.tensor_scalar(
                out=mdump[:, :], in0=scoresR[:, :], scalar1=sc8[:, 1:2], scalar2=0.0,
                op0=Op.is_ge, op1=Op.add, accum_out=sc8[:, 8:9]))
            M.op("vector", lambda E: E.drain())
            M.op("vector", lambda E: E.tensor_scalar(
                out=mA[:, :], in0=scoresR[:, :], scalar1=sc8[:, 1:2], scalar2=None, op0=Op.is_ge))
            v = M.op("vector", lambda E: E.tensor_scalar(
                out=mB[:, :], in0=scoresR[:, :], scalar1=sc8[:, 0:1], scalar2=None, op0=Op.is_ge), inc=sv)
            M.op("vector", lambda E: E.memset(onesS[:, :], 1.0))
            M.op("vector", lambda E: E.tensor_tensor(
                out=mB[:, :], in0=mB[:, :], in1=mA[:, :], op=Op.subtract))
            M.op("vector", lambda E: E.tensor_tensor_scan(
                out=rA[:, :], data0=onesS[:, :], data1=mA[:, :], initial=0.0,
                op0=Op.mult, op1=Op.add))
            M.op("vector", lambda E: E.drain())
            M.op("vector", lambda E: E.tensor_tensor(
                out=rA[:, :], in0=rA[:, :], in1=mA[:, :], op=Op.mult))
            M.op("vector", lambda E: E.tensor_tensor_scan(
                out=mA[:, :], data0=onesS[:, :], data1=mB[:, :], initial=0.0,
                op0=Op.mult, op1=Op.add))
            M.op("vector", lambda E: E.drain())
            M.op("vector", lambda E: E.drain())
            M.op("vector", lambda E: E.tensor_scalar(
                out=mA[:, :], in0=mA[:, :], scalar1=sc8[:, 8:9], scalar2=None, op0=Op.add))
            M.op("vector", lambda E: E.tensor_tensor(
                out=mA[:, :], in0=mA[:, :], in1=mB[:, :], op=Op.mult))
            M.op("vector", lambda E: E.drain())
            RC_RDY = M.op("vector", lambda E: E.tensor_tensor(
                out=onesS[:, :], in0=rA[:, :], in1=mA[:, :], op=Op.add), inc=sv)
            M.wait("sync", sv, RC_RDY)
            M.op("sync", lambda E: E.dma_start(out=rcomb_d[:, :], in_=onesS[:, :]), inc=do, k=16)
            DO_RC = do.n
            if "route" in dbg:
                tdbg = dout("dbg_sc8", [8, 12])
                M.wait("sync", sv, sv.n)
                M.op("sync", lambda E, d=tdbg: E.dma_start(out=d[:, :], in_=sc8[:, :]), inc=do, k=16)
            M.wait("gpsimd", do, DO_RC)
            for g in range(8):
                M.use("gpsimd", "rrep")
                vdg = M.op("gpsimd", lambda E, g=g: E.dma_start(
                    out=rrep[:, :], in_=rcomb_d[g:g + 1, :].to_broadcast([128, N_])), inc=dg, k=16)
                M.wait("vector", dg, vdg)
                v = M.op("vector", lambda E, g=g: E.max_index(
                    out=idxtab[:, g, :], in_max=findv[:, :], in_values=rrep[:, :]), inc=sv)
                M.free("rrep", sv, v)
                M.wait("gpsimd", sv, v)
                if g == 0:
                    M.wait("gpsimd", do, DO_SCORES)
                for j in range(4):
                    M.op("gpsimd", lambda E, g=g, j=j: E.indirect_dma_start(
                        out=wtab[:, g, j:j + 1], out_offset=None,
                        in_=scores_d[:, :],
                        in_offset=IndirectOffsetOnAxis(ap=idxtab[:, g, j:j + 1], axis=0),
                        element_offset=g * N_), inc=dg, k=16)
            DG_W = dg.n
            if "idx" in dbg:
                idbg = dout("dbg_idxtab", [128, 8 * 8], U32)
                wdbg = dout("dbg_wtab", [128, 8 * 4])
                M.wait("sync", dg, DG_W)
                M.wait("sync", sv, sv.n)
                M.op("sync", lambda E, d=idbg: E.dma_start(
                    out=d[:, :], in_=idxtab[:, :, :].rearrange("p a b -> p (a b)")), inc=do, k=16)
                M.op("sync", lambda E, d=wdbg: E.dma_start(
                    out=d[:, :], in_=wtab[:, :, :].rearrange("p a b -> p (a b)")), inc=do, k=16)

            M.wait("sync", sv, sv.n)
            M.op("sync", lambda E: E.dma_start(
                out=idxt_in[:, :].rearrange("(g p) j -> p g j", p=128),
                in_=idxtab[:, :, 0:4]), inc=do, k=16)
            M.wait("gpsimd", do, do.n)
            M.op("gpsimd", lambda E: E.collective_compute(
                "AllGather", Op.bypass, replica_groups=[list(range(NCORES))],
                ins=[idxt_in.opt()], outs=[idxt_all.opt()]), inc=cc)

        if on("c"):
            M.wait("gpsimd", dg, dg.n)
            for ei in range(E2):
                M.wait("gpsimd", st, st.n)
                M.op("gpsimd", lambda E, ei=ei: E.dma_start(
                    out=WgTe[:, :, :],
                    in_=WgTe_i[ei, :, :].rearrange("(a p) b -> p a b", p=128)), inc=dw, k=16)
                M.op("gpsimd", lambda E, ei=ei: E.dma_start(
                    out=WuTe[:, :, :],
                    in_=WuTe_i[ei, :, :].rearrange("(a p) b -> p a b", p=128)), inc=dw, k=16)
                M.op("gpsimd", lambda E, ei=ei: E.dma_start(
                    out=WdTe[:, :, :],
                    in_=WdTe_i[ei, :, :].rearrange("(a p) b -> p a b", p=128)), inc=dw, k=16)
                DW_E = dw.n
                M.wait("tensor", dw, DW_E)
                for bb in range(4):
                    g = 4 * ei + bb
                    M.use("vector", "xtokfm_w")
                    for j in range(4):
                        M.use("gpsimd", f"xtok{j % 2}")
                        vdg = M.op("gpsimd", lambda E, g=g, j=j, bb=bb: E.indirect_dma_start(
                            out=xtok[j % 2][:, :], out_offset=None, in_=x1all[:, :],
                            in_offset=IndirectOffsetOnAxis(ap=idxtab[:, g, j:j + 1], axis=0),
                            element_offset=bb * 4096 * 1024), inc=dg, k=16)
                        M.wait("tensor", dg, vdg)
                        pt = PTs[j % 2]
                        M.use("tensor", f"PT{j % 2}")
                        for dc in range(8):
                            M.op("tensor", lambda E, j=j, dc=dc, pt=pt: E.transpose(
                                pt[:, dc * 128:dc * 128 + 128],
                                xtok[j % 2][:, dc * 128:dc * 128 + 128], ident_b[:, :]),
                                inc=(st if dc == 7 else None))
                        M.free(f"xtok{j % 2}", st, st.n)
                        M.wait("vector", st, st.n)
                        v = M.op("vector", lambda E, j=j, pt=pt: E.tensor_copy(
                            out=xtokfm[:, :, j * 128:j * 128 + 128],
                            in_=pt[:, :].rearrange("p (a b) -> p a b", a=8)), inc=sv)
                        M.free(f"PT{j % 2}", sv, v)
                    XF_RDY = sv.n
                    M.wait("tensor", sv, XF_RDY)
                    M.use("scalar", "hfm_w")
                    for fc in range(16):
                        pb = (fc % 2) * 2
                        M.use("tensor", f"P{pb}", f"P{pb + 1}")
                        for dc in range(8):
                            M.op("tensor", lambda E, fc=fc, dc=dc, pb=pb: E.matmul(
                                P[pb][:, :], WgTe[:, dc, fc * 128:fc * 128 + 128],
                                xtokfm[:, dc, :], start=(dc == 0), stop=(dc == 7)))
                            M.op("tensor", lambda E, fc=fc, dc=dc, pb=pb: E.matmul(
                                P[pb + 1][:, :], WuTe[:, dc, fc * 128:fc * 128 + 128],
                                xtokfm[:, dc, :], start=(dc == 0), stop=(dc == 7)),
                                inc=(st if dc == 7 else None))
                        M.wait("scalar", st, st.n)
                        vs = M.op("scalar", lambda E, fc=fc, pb=pb: E.activation(
                            out=hfm[:, fc, :], in_=P[pb][:, :], func=AF.Gelu), inc=ss)
                        M.free(f"P{pb}", ss, vs)
                        M.wait("vector", ss, vs)
                        v = M.op("vector", lambda E, fc=fc, pb=pb: E.tensor_tensor(
                            out=hfm[:, fc, :], in0=hfm[:, fc, :], in1=P[pb + 1][:, :], op=Op.mult), inc=sv)
                        M.free(f"P{pb + 1}", sv, v)
                    H_RDY = sv.n
                    M.free("xtokfm_w", st, st.n)
                    M.wait("tensor", sv, H_RDY)
                    for tc in range(4):
                        for dmc in range(2):
                            bname = f"P{2 + dmc}"
                            bank = P[2 + dmc]
                            M.use("tensor", bname)
                            for fc in range(16):
                                M.op("tensor", lambda E, tc=tc, dmc=dmc, fc=fc, bank=bank: E.matmul(
                                    bank[:, :], hfm[:, fc, tc * 128:tc * 128 + 128],
                                    WdTe[:, fc, dmc * 512:dmc * 512 + 512],
                                    start=(fc == 0), stop=(fc == 15)),
                                    inc=(st if fc == 15 else None))
                        M.wait("vector", st, st.n)
                        M.use("vector", "eotile")
                        for dmc in range(2):
                            bank = P[2 + dmc]
                            v = M.op("vector", lambda E, tc=tc, dmc=dmc, bank=bank, g=g: E.tensor_scalar(
                                out=eotile[:, dmc * 512:dmc * 512 + 512], in0=bank[:, :],
                                scalar1=wtab[:, g, tc:tc + 1], scalar2=None, op0=Op.mult), inc=sv)
                            M.free(f"P{2 + dmc}", sv, v)
                        M.wait("sync", sv, sv.n)
                        M.op("sync", lambda E, g=g, tc=tc: E.dma_start(
                            out=eo_in[g * CAP + tc * 128:g * CAP + tc * 128 + 128, :],
                            in_=eotile[:, :]), inc=do, k=16)
                        M.free("eotile", do, do.n)
                    M.free("hfm_w", st, st.n)
            M.wait("gpsimd", do, do.n)
            M.op("gpsimd", lambda E: E.collective_compute(
                "AllGather", Op.bypass, replica_groups=[list(range(NCORES))],
                ins=[eo_in.opt()], outs=[eo_all.opt()]), inc=cc)
            CC_AG2 = cc.n
            M.wait("gpsimd", cc, CC_AG2)

        if on("d"):
            M.wait("gpsimd", st, st.n)
            M.wait("gpsimd", sv, sv.n)
            M.wait("gpsimd", do, do.n)
            DG_STAGE = dg.n
            for s in range(16):
                par = s % 2
                M.op("gpsimd", lambda E, s=s, par=par: E.indirect_dma_start(
                    out=idxsl[:, par * 4:par * 4 + 4], out_offset=None, in_=idxt_all[:, :],
                    in_offset=IndirectOffsetOnAxis(ap=idxr_s[:, s:s + 1], axis=0)), inc=dg, k=16)
                for j in range(4):
                    M.op("gpsimd", lambda E, s=s, j=j, par=par: E.indirect_dma_start(
                        out=eoD[par * 4 + j][:, :], out_offset=None, in_=eo_all[:, :],
                        in_offset=IndirectOffsetOnAxis(ap=eor_s[:, s * 4 + j:s * 4 + j + 1], axis=0)), inc=dg, k=16)
                DG_LOAD = dg.n
                M.wait("gpsimd", dg, DG_STAGE)
                M.wait("gpsimd", dg, DG_LOAD)
                for j in range(4):
                    M.op("gpsimd", lambda E, j=j, par=par: E.indirect_dma_start(
                        out=acc_d[:, :],
                        out_offset=IndirectOffsetOnAxis(ap=idxsl[:, par * 4 + j:par * 4 + j + 1], axis=0),
                        in_=eoD[par * 4 + j][:, :], in_offset=None,
                        compute_op=Op.add), inc=dg, k=16)
                DG_STAGE = dg.n
            for fc in range(16):
                M.use("gpsimd", "accf")
                vdg = M.op("gpsimd", lambda E, fc=fc: E.indirect_dma_start(
                    out=accf[:, :], out_offset=None, in_=acc_d[:, :],
                    in_offset=IndirectOffsetOnAxis(ap=accr_s[:, fc:fc + 1], axis=0)), inc=dg, k=16)
                M.wait("vector", dg, vdg)
                vpre = M.op("vector", lambda E: E.tensor_reduce(
                    out=osc[:, 4:5], in_=accf[:, :], axis=mybir.AxisListType.X, op=Op.add), inc=sv)
                M.wait("scalar", sv, vpre)
                vsq = M.op("scalar", lambda E: E.square(lnscr[:, :], accf[:, :]), inc=ss)
                M.wait("vector", ss, vsq)
                M.op("vector", lambda E: E.tensor_reduce(
                    out=osc[:, 5:6], in_=lnscr[:, :], axis=mybir.AxisListType.X, op=Op.add))
                M.op("vector", lambda E: E.drain())
                M.op("vector", lambda E: E.tensor_scalar(
                    out=osc[:, 6:7], in0=osc[:, 4:5], scalar1=-1.0 / D_, scalar2=None, op0=Op.mult))
                M.op("vector", lambda E: E.tensor_scalar(
                    out=osc[:, 7:8], in0=osc[:, 5:6], scalar1=1.0 / D_, scalar2=EPS, op0=Op.mult, op1=Op.add))
                M.op("vector", lambda E: E.tensor_tensor(
                    out=osc[:, 8:9], in0=osc[:, 6:7], in1=osc[:, 6:7], op=Op.mult))
                M.op("vector", lambda E: E.drain())
                v = M.op("vector", lambda E: E.tensor_tensor(
                    out=osc[:, 10:11], in0=osc[:, 7:8], in1=osc[:, 8:9], op=Op.subtract), inc=sv)
                M.op("vector", lambda E: E.drain())
                M.wait("scalar", sv, v)
                v = M.op("scalar", lambda E: E.sqrt(osc[:, 11:12], osc[:, 10:11]), inc=ss)
                M.wait("vector", ss, v)
                M.op("vector", lambda E: E.reciprocal(osc[:, 12:13], osc[:, 11:12]))
                M.op("vector", lambda E: E.drain())
                M.use("vector", "outt")
                v = M.op("vector", lambda E: E.tensor_scalar(
                    out=outt[:, :], in0=accf[:, :], scalar1=osc[:, 6:7], scalar2=osc[:, 12:13],
                    op0=Op.add, op1=Op.mult), inc=sv)
                M.free("accf", sv, v)
                M.wait("sync", sv, v)
                M.op("sync", lambda E, fc=fc: E.dma_start(
                    out=out_loc[fc * 128:fc * 128 + 128, :], in_=outt[:, :]), inc=do, k=16)
                M.free("outt", do, do.n)
        else:
            M.wait("vector", sv, sv.n)
            M.op("vector", lambda E: E.memset(outt[:, :], 0.0), inc=sv)
            M.wait("sync", sv, sv.n)
            for fc in range(16):
                M.op("sync", lambda E, fc=fc: E.dma_start(
                    out=out_loc[fc * 128:fc * 128 + 128, :], in_=outt[:, :]), inc=do, k=16)

        M.wait("gpsimd", do, do.n)
        M.wait("gpsimd", dg, dg.n)
        M.wait("gpsimd", dx, dx.n)
        M.wait("gpsimd", dw, dw.n)

        @block.sync
        def _(E):
            for f in M.q["sync"]:
                f(E)

        @block.vector
        def _(E):
            for f in M.q["vector"]:
                f(E)

        @block.scalar
        def _(E):
            for f in M.q["scalar"]:
                f(E)

        @block.tensor
        def _(E):
            for f in M.q["tensor"]:
                f(E)

        @block.gpsimd
        def _(E):
            for f in M.q["gpsimd"]:
                f(E)

    es.close()
    return nc, dbg_t


_BUILD_CACHE = {}


def _get_nc(upto="full", dbg=()):
    key = (upto, tuple(sorted(dbg)))
    if key not in _BUILD_CACHE:
        _BUILD_CACHE[key] = build_nc(upto, dbg)
    return _BUILD_CACHE[key]


def make_in_maps(inputs):
    sh = _host_shared()
    f32 = lambda a: np.ascontiguousarray(np.asarray(a), dtype=np.float32)
    x = f32(inputs["L0_attn_x"])
    Wq = f32(inputs["L0_attn_Wq"]); Wk = f32(inputs["L0_attn_Wk"])
    Wv = f32(inputs["L0_attn_Wv"]); Wg = f32(inputs["L0_attn_Wg"]); Wo = f32(inputs["L0_attn_Wo"])
    gn = f32(inputs["L0_attn_gn_weight"])
    gate_w = f32(inputs["L0_ffn_gate_w"])
    WgE = f32(inputs["L0_ffn_Wg"]); WuE = f32(inputs["L0_ffn_Wu"]); WdE = f32(inputs["L0_ffn_Wd"])
    Wsg = f32(inputs["L0_ffn_Ws_g"]); Wsu = f32(inputs["L0_ffn_Ws_u"]); Wsd = f32(inputs["L0_ffn_Ws_d"])

    shared = {
        "Wq": Wq, "Wk": Wk, "Wv": Wv, "Wg": Wg, "Wo": Wo,
        "gn_bc": np.ascontiguousarray(np.broadcast_to(np.tile(gn, 1)[None, :], (128, DV))),
        "gatewT": np.ascontiguousarray(gate_w.T),
        "WsgT": np.ascontiguousarray(Wsg.T), "WsuT": np.ascontiguousarray(Wsu.T),
        "WsdT": np.ascontiguousarray(Wsd.T),
        "ident": sh["ident"], "dmaskT": sh["dmaskT"],
        "crossk": sh["crossk"], "crossq": sh["crossq"],
        "ones16": sh["ones16"], "findvals": sh["findvals"],
    }
    in_maps = []
    for c in range(NCORES):
        b, hf = c // 2, c % 2
        pc = _host_percore(c)
        m = dict(shared)
        m["x_loc"] = np.ascontiguousarray(x[b, hf * NT:(hf + 1) * NT, :])
        m["cos_t"] = np.ascontiguousarray(sh["cos_full"][:, hf * NT:(hf + 1) * NT])
        m["sin_t"] = np.ascontiguousarray(sh["sin_full"][:, hf * NT:(hf + 1) * NT])
        m["WgTe"] = np.ascontiguousarray(np.stack([WgE[2 * c + i].T for i in range(E2)]))
        m["WuTe"] = np.ascontiguousarray(np.stack([WuE[2 * c + i].T for i in range(E2)]))
        m["WdTe"] = np.ascontiguousarray(np.stack([WdE[2 * c + i].T for i in range(E2)]))
        for k in ("gate", "sinit_rows", "explog_rows", "den_rows", "eo_rows",
                  "idx_rows", "acc_rows"):
            m[k] = pc[k]
        in_maps.append(m)
    return in_maps


def kernel(**inputs):
    nc, _ = _get_nc("full")
    in_maps = make_in_maps(inputs)
    res = run_bass_kernel_spmd(nc, in_maps, core_ids=list(range(NCORES)))
    out = np.zeros((B_, N_, D_), dtype=np.float32)
    for c in range(NCORES):
        b, hf = c // 2, c % 2
        out[b, hf * NT:(hf + 1) * NT, :] = res.results[c]["out_loc"]
    return out


# version 18
# speedup vs baseline: 1.0082x; 1.0082x over previous
import contextlib
import numpy as np
import ml_dtypes
import concourse.bass as bass
import concourse.mybir as mybir
from concourse.bass_utils import run_bass_kernel_spmd
from concourse.bass import IndirectOffsetOnAxis
from concourse.alu_op_type import AluOpType as Op

F32 = mybir.dt.float32
F16 = mybir.dt.float16
BF16 = mybir.dt.bfloat16
U32 = mybir.dt.uint32
AF = mybir.ActivationFunctionType

NCORES = 8
B_, N_, D_ = 4, 4096, 1024
H, DK, DV = 4, 256, 512
VDIM = 2048
NT = 2048
TIL = 512
NTIL = NT // TIL
CH = 256
E2 = 2
CAP = 512
FE = 2048
FS = 2048
EPS = 1e-5
ROPE_BASE = 10000.0
BIS_IT = 30
KB = 1024

GAMMA = 1.0 - np.exp2(-5.0 - np.arange(H).astype(np.float64))
GC = [float(g ** CH) for g in GAMMA]


def _host_shared():
    d = {}
    d["ident"] = np.eye(128, dtype=np.float32)
    inv_freq = 1.0 / ROPE_BASE ** (np.arange(0, DK, 2, dtype=np.float64) / DK)
    t = np.arange(N_, dtype=np.float64)
    fr = np.outer(inv_freq, t)
    d["cos_full"] = np.cos(fr).astype(np.float32)
    d["sin_full"] = np.sin(fr).astype(np.float32)
    dm = np.zeros((128, H, 2, CH), dtype=np.float32)
    for h in range(H):
        g = GAMMA[h]
        i = np.arange(CH)[None, :]
        for jc in range(2):
            j = (jc * 128 + np.arange(128))[:, None]
            rel = i - j
            dm[:, h, jc, :] = np.where(rel >= 0, g ** np.maximum(rel, 0), 0.0) / 16.0
    d["dmaskT"] = dm.reshape(128, H * 2 * CH).copy()
    ck = np.zeros((128, H, 2), dtype=np.float32)
    cq = np.zeros((128, H, 2), dtype=np.float32)
    for h in range(H):
        g = GAMMA[h]
        for sub in range(2):
            pos = sub * 128 + np.arange(128)
            ck[:, h, sub] = g ** (CH - 1 - pos)
            cq[:, h, sub] = (g ** (pos + 1)) / 16.0
    d["crossk"] = ck.reshape(128, H * 2).copy()
    d["crossq"] = cq.reshape(128, H * 2).copy()
    d["ones16"] = np.ones((16, 1), dtype=np.float32)
    fv = np.zeros((128, 8), dtype=np.float32)
    for p in range(128):
        for j in range(8):
            fv[p, j] = (j + 1) if p == 0 else ((4 * p + j + 1) if j < 4 else (j + 1))
    d["findvals"] = fv
    return d


def _host_percore(c):
    b, hf = c // 2, c % 2
    d = {}
    d["gate"] = np.full((128, 1), 1.0 if hf == 1 else 0.0, dtype=np.float32)
    partner = c - 1 if hf == 1 else c
    d["sinit_rows"] = (partner * 1024 + (np.arange(8)[None, :] * 128
                                         + np.arange(128)[:, None])).astype(np.uint32)
    er = np.zeros((8, 2), np.uint32)
    dr = np.zeros((8, 2), np.uint32)
    for ei in range(E2):
        e = 2 * c + ei
        for bb in range(4):
            g = 4 * ei + bb
            for hh in range(2):
                er[g, hh] = 16 * (2 * bb + hh) + e
                dr[g, hh] = 2 * bb + hh
    d["explog_rows"] = er
    d["den_rows"] = dr
    eor = np.zeros((128, 64), np.uint32)
    idxr = np.zeros((128, 16), np.uint32)
    p = np.arange(128)
    s = 0
    for sc in range(8):
        for ei in range(E2):
            g = 4 * ei + b
            for j in range(4):
                eor[:, s * 4 + j] = (sc * 8 + g) * CAP + j * 128 + p
            idxr[:, s] = (sc * 8 + g) * 128 + p
            s += 1
    d["eo_rows"] = eor
    d["idx_rows"] = idxr
    d["acc_rows"] = (2048 * hf + (np.arange(16)[None, :] * 128
                                  + p[:, None])).astype(np.uint32)
    return d


class Sem:
    def __init__(self, h):
        self.h = h
        self.n = 0

    def bump(self, k=1):
        self.n += k
        return self.n


class MK:
    ENGS = ("sync", "vector", "scalar", "tensor", "gpsimd")

    def __init__(self, nc, arena_pb):
        self.nc = nc
        self.q = {e: [] for e in self.ENGS}
        self.arena_pb = arena_pb
        self.overlays = []
        self.res = {}

    def op(self, eng, f, inc=None, k=1):
        if inc is not None:
            val = inc.bump(k)
            self.q[eng].append(lambda E, f=f, s=inc.h, k=k: f(E).then_inc(s, k))
            return val
        self.q[eng].append(f)
        return None

    def wait(self, eng, sem, val):
        assert val <= sem.n, f"wait on future value {val} > {sem.n}"
        if val > 0:
            self.q[eng].append(lambda E, s=sem.h, v=val: E.wait_ge(s, v))

    def use(self, eng, *names):
        for nm in names:
            s = self.res.get(nm)
            if s is not None:
                self.wait(eng, s[0], s[1])

    def free(self, nm, sem, val=None):
        self.res[nm] = (sem, sem.n if val is None else val)

    def sb(self, name, shape, dtype, off_b):
        nbytes = int(np.prod(shape[1:])) * mybir.dt.size(dtype)
        assert off_b % 32 == 0 and off_b + nbytes <= self.arena_pb, (name, off_b, nbytes)
        h = self.nc.virtual_sbuf_tensor(name, list(shape), dtype)
        self.overlays.append((h, off_b))
        return h

    def finalize(self, arena_handle):
        base = self.nc.lookup_mloc(arena_handle).addr
        for h, off in self.overlays:
            self.nc._tensor(h.name, h.shape, h.dtype, type="SB")
            mloc = self.nc.lookup_mloc(h)
            mloc.allocated = True
            mloc.addr = base + off


ARENA_PB = 176 * KB
A2B_SUB = 9


def build_nc(upto="full", dbg=()):
    dbg = set(dbg)
    PHASES = ["p1", "p2", "a2b", "broute", "c", "d", "full"]
    lim = PHASES.index(upto)

    def on(ph):
        return PHASES.index(ph) <= lim

    nc = bass.Bass("TRN2", target_bir_lowering=False)

    def din(name, shape, dt=F32):
        return nc.dram_tensor(name, list(shape), dt, kind="ExternalInput").ap()

    def dint(name, shape, dt=F32):
        return nc.dram_tensor(name, list(shape), dt).ap()

    x_loc = din("x_loc", [NT, D_])
    Wq_i = din("Wq", [D_, D_]); Wk_i = din("Wk", [D_, D_])
    Wv_i = din("Wv", [D_, VDIM]); Wg_i = din("Wg", [D_, VDIM]); Wo_i = din("Wo", [VDIM, D_])
    gn_i = din("gn_bc", [128, DV])
    gatewT_i = din("gatewT", [D_, 16])
    WsgT_i = din("WsgT", [D_, FS]); WsuT_i = din("WsuT", [D_, FS]); WsdT_i = din("WsdT", [FS, D_])
    WgTe_i = din("WgTe", [E2, D_, FE]); WuTe_i = din("WuTe", [E2, D_, FE])
    WdTe_i = din("WdTe", [E2, FE, D_])
    ident_i = din("ident", [128, 128])
    cos_i = din("cos_t", [128, NT]); sin_i = din("sin_t", [128, NT])
    dmask_i = din("dmaskT", [128, H * 2 * CH])
    crossk_i = din("crossk", [128, H * 2]); crossq_i = din("crossq", [128, H * 2])
    ones16_i = din("ones16", [16, 1])
    findv_i = din("findvals", [128, 8])
    gate_i = din("gate", [128, 1])
    sinitr_i = din("sinit_rows", [128, 8], U32)
    elr_i = din("explog_rows", [8, 2], U32)
    denr_i = din("den_rows", [8, 2], U32)
    eor_i = din("eo_rows", [128, 64], U32)
    idxr_i = din("idx_rows", [128, 16], U32)
    accr_i = din("acc_rows", [128, 16], U32)

    out_loc = nc.dram_tensor("out_loc", [NT, D_], F32, kind="ExternalOutput").ap()
    dbg_t = {}

    def dout(name, shape, dt=F32):
        ap = nc.dram_tensor(name, list(shape), dt, kind="ExternalOutput").ap()
        dbg_t[name] = ap
        return ap

    spill_v = dint("spill_v", [NT, VDIM], BF16)
    spill_kc = dint("spill_kc", [H, NT, DK], BF16)
    spill_kr = dint("spill_kr", [H * DK, NT], BF16)
    spill_on = dint("spill_on", [NT, VDIM], BF16)
    ag_s_in = dint("ag_s_in", [1024, DV])
    ag_s_out = dint("ag_s_out", [NCORES * 1024, DV])
    x1b_in = dint("x1b_in", [NT, D_], BF16)
    x1all = dint("x1all", [NCORES * NT, D_], BF16)
    x1f_d = dint("x1f", [NT, D_])
    explog_in = dint("explog_in", [16, NT])
    explog_out = dint("explog_out", [8 * 16, NT])
    den_in = dint("den_in", [1, NT])
    den_out = dint("den_out", [8, NT])
    scores_d = dint("scores_d", [8 * N_, 1])
    rcomb_d = dint("rcomb_d", [8, N_], F16)
    acc_d = dint("acc", [N_, D_])
    eo_in = dint("eo_in", [8 * CAP, D_], BF16)
    eo_all = dint("eo_all", [NCORES * 8 * CAP, D_], BF16)
    idxt_in = dint("idxt_in", [8 * 128, 4], U32)
    idxt_all = dint("idxt_all", [NCORES * 8 * 128, 4], U32)

    es = contextlib.ExitStack()
    sb = lambda n, s, d: es.enter_context(nc.sbuf_tensor(n, list(s), d))

    ident_b = sb("ident_b", [128, 128], BF16)
    cossin = sb("cossin", [128, 2, NT], F16)
    dmaskT = sb("dmaskT_s", [128, H, 2, CH], BF16)
    crossk = sb("crossk_s", [128, H, 2], F32)
    crossq = sb("crossq_s", [128, H, 2], F32)
    gn_b = sb("gn_b", [128, DV], F32)
    gate_s = sb("gate_s", [128, 1], F32)
    ones16 = sb("ones16_s", [16, 1], F32)
    findv = sb("findv", [128, 8], F16)
    idxr_s = sb("idxr_s", [128, 16], U32)
    sinitr = sb("sinitr", [128, 8], U32)
    eor_s = sb("eor_s", [128, 64], U32)
    accr_s = sb("accr_s", [128, 16], U32)
    elr_s = sb("elr_s", [8, 2], U32)
    denr_s = sb("denr_s", [8, 2], U32)
    osc = sb("osc", [128, 16], F32)
    wgate = sb("wgate", [128, 8, 16], BF16)
    idxtab = sb("idxtab", [128, 8, 8], U32)
    wtab = sb("wtab", [128, 8, 4], F32)
    sc8 = sb("sc8", [8, 12], F32)
    explg = sb("explg", [33, 512], F32)
    arena = sb("arena", [128, ARENA_PB], mybir.dt.uint8)

    P = [es.enter_context(nc.psum_tensor(f"P{i}", [128, 512], F32)) for i in range(6)]
    PT0 = es.enter_context(nc.psum_tensor("PT0", [128, 1024], BF16))
    PT1 = es.enter_context(nc.psum_tensor("PT1", [128, 1024], BF16))
    PTs = [PT0, PT1]

    sems = {k: es.enter_context(nc.semaphore("s_" + k))
            for k in ("dx", "do", "dw", "dg", "v", "s", "t", "cc")}

    M = MK(nc, ARENA_PB)
    dx, do, dw, dg = Sem(sems["dx"]), Sem(sems["do"]), Sem(sems["dw"]), Sem(sems["dg"])
    sv, ss, st, cc = Sem(sems["v"]), Sem(sems["s"]), Sem(sems["t"]), Sem(sems["cc"])

    xfm = M.sb("xfm", [128, 8, NT], BF16, 0)
    x1btm = M.sb("x1btm", [128, 2, 1024], BF16, 0)
    x1fmb = M.sb("x1fmb", [128, 8, 256], BF16, 4 * KB)
    hsb = M.sb("hsb", [128, 16, 256], BF16, 8 * KB)
    x1frel = M.sb("x1frel", [128, 2, 1024], F32, 16 * KB)
    mdump = M.sb("mdump", [8, N_], F16, 24 * KB)
    xtok = [M.sb(f"xtok{j}", [128, 1024], BF16, j * 2 * KB) for j in range(2)]
    xtokfm = M.sb("xtokfm", [128, 8, 512], BF16, 4 * KB)
    hfm = M.sb("hfm", [128, 16, 512], BF16, 12 * KB)
    eotile = M.sb("eotile", [128, 1024], BF16, 28 * KB)
    eoD = [M.sb(f"eoD{j}", [128, 1024], BF16, j * 2 * KB) for j in range(8)]
    idxsl = M.sb("idxsl", [128, 8], U32, 16 * KB)
    accf = M.sb("accf", [128, 1024], F32, 18 * KB)
    outt = M.sb("outt", [128, 1024], F32, 22 * KB)
    lnscr = M.sb("lnscr", [128, 1024], BF16, 26 * KB)
    WkA = M.sb("WkA", [128, 8, 1024], BF16, 32 * KB)
    ogat_b = M.sb("ogat_b", [128, 4, VDIM], BF16, 32 * KB)
    rrep = M.sb("rrep", [128, 4096], F16, 32 * KB)
    WvB = M.sb("WvB", [128, 8, VDIM], BF16, 48 * KB)
    ogat = M.sb("ogat", [128, 4, VDIM], BF16, 48 * KB)
    attnT = M.sb("attnT", [128, 2, CH], BF16, 64 * KB)
    obuf = M.sb("obuf", [128, DV], F32, 65 * KB)
    osq = M.sb("osq", [128, DV], F32, 67 * KB)
    WgB = M.sb("WgB", [128, 8, VDIM], BF16, 48 * KB)
    WsgT = M.sb("WsgT", [128, 8, FS], BF16, 48 * KB)
    WgTe = M.sb("WgTe", [128, 8, FE], BF16, 48 * KB)
    Sm = M.sb("Sm", [128, H, 2, DV], F32, 80 * KB)
    Sb = M.sb("Sb", [128, H, 2, DV], BF16, 96 * KB)
    WoC = M.sb("WoC", [128, 16, 1024], BF16, 80 * KB)
    WsuT = M.sb("WsuT", [128, 8, FS], BF16, 80 * KB)
    WuTe = M.sb("WuTe", [128, 8, FE], BF16, 80 * KB)
    xin = M.sb("xin", [128, 4, 1024], F32, 112 * KB)
    ktile = M.sb("ktile", [128, 8, TIL], BF16, 128 * KB)
    vtile = M.sb("vtile", [128, 4, VDIM], BF16, 136 * KB)
    kctile = M.sb("kctile", [128, 4, H, 256], BF16, 152 * KB)
    xbf = M.sb("xbf", [128, 4, 1024], BF16, 160 * KB)
    rope1 = M.sb("rope1", [128, TIL], F32, 168 * KB)
    rope2 = M.sb("rope2", [128, TIL], F32, 170 * KB)
    krt2 = M.sb("krt2", [128, 8, TIL], BF16, 112 * KB)
    ogat2 = M.sb("ogat2", [128, 4, VDIM], BF16, 128 * KB)
    ogfm = M.sb("ogfm", [128, 16, TIL], BF16, 144 * KB)
    x1pre = M.sb("x1pre", [128, 1024], F32, 160 * KB)
    x1btile = M.sb("x1btile", [128, 1024], BF16, 164 * KB)
    x1fmt = M.sb("x1fmt", [128, 8, TIL], BF16, 166 * KB)
    osqb = M.sb("osqb", [128, 1024], BF16, 174 * KB)
    WsdT = M.sb("WsdT", [128, 16, 1024], BF16, 112 * KB)
    scoresR = M.sb("scoresR", [8, N_], F32, 144 * KB)
    denT = M.sb("denT", [8, N_], F32, 160 * KB)
    mA = M.sb("mA", [8, N_], F16, 160 * KB)
    mB = M.sb("mB", [8, N_], F16, 168 * KB)
    onesS = M.sb("onesS", [8, N_], F16, 144 * KB)
    rA = M.sb("rA", [8, N_], F16, 152 * KB)
    WdTe = M.sb("WdTe", [128, 16, 1024], BF16, 112 * KB)

    with nc.Block() as block:
        M.finalize(arena)

        plain = [(crossk[:, :, :].rearrange("p a b -> p (a b)"), crossk_i[:, :]),
                 (crossq[:, :, :].rearrange("p a b -> p (a b)"), crossq_i[:, :]),
                 (gn_b[:, :], gn_i[:, :]), (gate_s[:, :], gate_i[:, :]),
                 (idxr_s[:, :], idxr_i[:, :]), (sinitr[:, :], sinitr_i[:, :]),
                 (eor_s[:, :], eor_i[:, :]), (accr_s[:, :], accr_i[:, :]),
                 (elr_s[:, :], elr_i[:, :]), (denr_s[:, :], denr_i[:, :]),
                 (ones16[:, :], ones16_i[:, :])]
        for d, s in plain:
            M.op("sync", lambda E, d=d, s=s: E.dma_start(out=d, in_=s), inc=dx, k=16)
        casts = [
            (ident_b[:, :], ident_i[:, :]),
            (dmaskT[:, :, :, :].rearrange("p a b c -> p (a b c)"), dmask_i[:, :]),
            (cossin[:, 0, :], cos_i[:, :]),
            (cossin[:, 1, :], sin_i[:, :]),
            (findv[:, :], findv_i[:, :]),
            (wgate[:, :, :], gatewT_i[:, :].rearrange("(a p) b -> p a b", p=128)),
            (WkA[:, :, :], Wk_i[:, :].rearrange("(a p) b -> p a b", p=128)),
            (WvB[:, :, :], Wv_i[:, :].rearrange("(a p) b -> p a b", p=128)),
        ]
        for d, s in casts:
            M.op("gpsimd", lambda E, d=d, s=s: E.dma_start(out=d, in_=s), inc=dw, k=16)
        DW_P1 = dw.n
        DX_CONST = dx.n
        M.wait("vector", dx, DX_CONST)
        M.wait("vector", dw, DW_P1)
        M.op("vector", lambda E: E.memset(Sm[:, :, :, :].rearrange("p a b c -> p (a b c)"), 0.0), inc=sv)
        SM_ZERO = sv.n

        def emit_xfm_tile(tt):
            T0 = tt * TIL
            M.use("sync", "xin")
            M.op("sync", lambda E, T0=T0: E.dma_start(
                out=xin[:, :, :],
                in_=x_loc[T0:T0 + TIL, :].rearrange("(s p) d -> p s d", p=128)), inc=dx, k=16)
            M.wait("vector", dx, dx.n)
            M.use("vector", "xbf")
            v = M.op("vector", lambda E: E.tensor_copy(out=xbf[:, :, :], in_=xin[:, :, :]), inc=sv)
            M.free("xin", sv, v)
            M.wait("tensor", sv, v)
            for sc in range(4):
                pt = PTs[sc % 2]
                M.use("tensor", f"PT{sc % 2}")
                for dc in range(8):
                    M.op("tensor", lambda E, sc=sc, dc=dc, pt=pt: E.transpose(
                        pt[:, dc * 128:dc * 128 + 128],
                        xbf[:, sc, dc * 128:dc * 128 + 128], ident_b[:, :]),
                        inc=(st if dc == 7 else None))
                M.wait("vector", st, st.n)
                v = M.op("vector", lambda E, sc=sc, pt=pt, T0=T0: E.tensor_copy(
                    out=xfm[:, :, T0 + sc * 128:T0 + sc * 128 + 128],
                    in_=pt[:, :].rearrange("p (a b) -> p a b", a=8)), inc=sv)
                M.free(f"PT{sc % 2}", sv, v)
            M.free("xbf", sv, sv.n)
            return sv.n

        def emit_proj_rope(tt, W_sb, dst_tile, spill_dst, dname, xrdy=None):
            T0 = tt * TIL
            M.wait("tensor", sv, sv.n if xrdy is None else xrdy)
            for h in range(H):
                pb = (h % 2) * 2
                M.use("tensor", f"P{pb}", f"P{pb + 1}")
                for kc2 in range(2):
                    kc = 2 * h + kc2
                    for dc in range(8):
                        M.op("tensor", lambda E, kc2=kc2, kc=kc, dc=dc, T0=T0, pb=pb: E.matmul(
                            P[pb + kc2][:, :], W_sb[:, dc, kc * 128:kc * 128 + 128],
                            xfm[:, dc, T0:T0 + TIL], start=(dc == 0), stop=(dc == 7)),
                            inc=(st if (dc == 7 and kc2 == 1) else None))
                M.wait("vector", st, st.n)
                if h == 0:
                    M.use("vector", dname, dname + "_do")
                cosb = cossin[:, 0, T0:T0 + TIL]
                sinb = cossin[:, 1, T0:T0 + TIL]
                M.op("vector", lambda E, cosb=cosb, pb=pb: E.tensor_tensor(out=rope1[:, :], in0=P[pb][:, :], in1=cosb, op=Op.mult))
                M.op("vector", lambda E, sinb=sinb, pb=pb: E.tensor_tensor(out=rope2[:, :], in0=P[pb + 1][:, :], in1=sinb, op=Op.mult))
                M.op("vector", lambda E, h=h, d=dst_tile: E.tensor_tensor(out=d[:, 2 * h, :], in0=rope1[:, :], in1=rope2[:, :], op=Op.subtract))
                M.op("vector", lambda E, cosb=cosb, pb=pb: E.tensor_tensor(out=rope1[:, :], in0=P[pb + 1][:, :], in1=cosb, op=Op.mult))
                M.op("vector", lambda E, sinb=sinb, pb=pb: E.tensor_tensor(out=rope2[:, :], in0=P[pb][:, :], in1=sinb, op=Op.mult))
                v = M.op("vector", lambda E, h=h, d=dst_tile: E.tensor_tensor(
                    out=d[:, 2 * h + 1, :], in0=rope1[:, :], in1=rope2[:, :], op=Op.add), inc=sv)
                M.free(f"P{pb}", sv, v)
                M.free(f"P{pb + 1}", sv, v)
            if spill_dst is not None:
                M.wait("scalar", sv, sv.n)
                M.op("scalar", lambda E, T0=T0, d=dst_tile: E.dma_start(
                    out=spill_dst[:, T0:T0 + TIL].rearrange("(a p) t -> p a t", p=128),
                    in_=d[:, :, :]), inc=do, k=16)
                M.free(dname + "_do", do, do.n)

        def emit_kcross(tt):
            T0 = tt * TIL
            M.use("vector", "kctile", "kctile_do")
            M.wait("tensor", sv, sv.n)
            for h in range(H):
                pt = PTs[h % 2]
                M.use("tensor", f"PT{h % 2}")
                for sc in range(4):
                    for kc2 in range(2):
                        M.op("tensor", lambda E, h=h, sc=sc, kc2=kc2, pt=pt: E.transpose(
                            pt[:, (sc * 2 + kc2) * 128:(sc * 2 + kc2) * 128 + 128],
                            ktile[:, 2 * h + kc2, sc * 128:sc * 128 + 128], ident_b[:, :]),
                            inc=(st if (sc == 3 and kc2 == 1) else None))
                M.wait("vector", st, st.n)
                for par in range(2):
                    v = M.op("vector", lambda E, h=h, par=par, pt=pt: E.tensor_scalar(
                        out=kctile[:, par::2, h, :],
                        in0=pt[:, :].rearrange("p (s c) -> p s c", s=4)[:, par::2, :],
                        scalar1=crossk[:, h, par:par + 1], scalar2=None, op0=Op.mult), inc=sv)
                M.free(f"PT{h % 2}", sv, v)
            M.wait("scalar", sv, sv.n)
            for h in range(H):
                M.op("scalar", lambda E, T0=T0, h=h: E.dma_start(
                    out=spill_kc[h, T0:T0 + TIL, :].rearrange("(s p) k -> p s k", p=128),
                    in_=kctile[:, :, h, :]), inc=do, k=16)
            M.free("kctile_do", do, do.n)

        def emit_vproj(tt):
            T0 = tt * TIL
            M.use("vector", "vtile", "vtile_do")
            for i, (sc, vc) in enumerate([(s, v) for s in range(4) for v in range(4)]):
                bname = f"P{2 + (i % 2)}"
                bank = P[2 + (i % 2)]
                M.use("tensor", bname)
                for dc in range(8):
                    M.op("tensor", lambda E, sc=sc, vc=vc, dc=dc, bank=bank, T0=T0: E.matmul(
                        bank[:, :], xfm[:, dc, T0 + sc * 128:T0 + sc * 128 + 128],
                        WvB[:, dc, vc * 512:vc * 512 + 512], start=(dc == 0), stop=(dc == 7)),
                        inc=(st if dc == 7 else None))
                M.wait("vector", st, st.n)
                v = M.op("vector", lambda E, sc=sc, vc=vc, bank=bank: E.tensor_copy(
                    out=vtile[:, sc, vc * 512:vc * 512 + 512], in_=bank[:, :]), inc=sv)
                M.free(bname, sv, v)
            M.wait("scalar", sv, sv.n)
            M.op("scalar", lambda E, T0=T0: E.dma_start(
                out=spill_v[T0:T0 + TIL, :].rearrange("(s p) d -> p s d", p=128),
                in_=vtile[:, :, :]), inc=do, k=16)
            M.free("vtile_do", do, do.n)

        def emit_retention(tt, with_o):
            M.wait("tensor", sv, sv.n)
            for cl in range(2):
                for h in range(H):
                    if with_o:
                        M.use("tensor", "P2")
                        for jc in range(2):
                            for kc2 in range(2):
                                M.op("tensor", lambda E, h=h, cl=cl, jc=jc, kc2=kc2: E.matmul(
                                    P[2][:, jc * 256:jc * 256 + 256],
                                    krt2[:, 2 * h + kc2, cl * 256 + jc * 128:cl * 256 + jc * 128 + 128],
                                    ktile[:, 2 * h + kc2, cl * 256:cl * 256 + 256],
                                    start=(kc2 == 0), stop=(kc2 == 1)),
                                    inc=(st if (jc == 1 and kc2 == 1) else None))
                        M.wait("vector", st, st.n)
                        M.use("vector", "attnT")
                        for jc in range(2):
                            v = M.op("vector", lambda E, h=h, jc=jc: E.tensor_tensor(
                                out=attnT[:, jc, :], in0=P[2][:, jc * 256:jc * 256 + 256],
                                in1=dmaskT[:, h, jc, :], op=Op.mult), inc=sv)
                        M.free("P2", sv, v)
                        M.wait("tensor", sv, v)
                        M.use("tensor", "P0", "P1", "P2", "P3", "Sb_w")
                        for ic in range(2):
                            bi = P[0] if ic == 0 else P[2]
                            bI = P[1] if ic == 0 else P[3]
                            for jc in range(2):
                                M.op("tensor", lambda E, h=h, cl=cl, ic=ic, jc=jc, bi=bi: E.matmul(
                                    bi[:, :], attnT[:, jc, ic * 128:ic * 128 + 128],
                                    vtile[:, cl * 2 + jc, h * 512:h * 512 + 512],
                                    start=(jc == 0), stop=(jc == 1)))
                            for kc2 in range(2):
                                M.op("tensor", lambda E, h=h, cl=cl, ic=ic, kc2=kc2, bI=bI: E.matmul(
                                    bI[:, :], ktile[:, 2 * h + kc2, cl * 256 + ic * 128:cl * 256 + ic * 128 + 128],
                                    Sb[:, h, kc2, :], start=(kc2 == 0), stop=(kc2 == 1)),
                                    inc=(st if (ic == 1 and kc2 == 1) else None))
                        ST_O = st.n
                        M.free("attnT", st, ST_O)
                    M.use("tensor", "P4", "P5")
                    for kc2 in range(2):
                        for jc in range(2):
                            M.op("tensor", lambda E, h=h, cl=cl, kc2=kc2, jc=jc: E.matmul(
                                P[4 + kc2][:, :], kctile[:, cl * 2 + jc, h, kc2 * 128:kc2 * 128 + 128],
                                vtile[:, cl * 2 + jc, h * 512:h * 512 + 512],
                                start=(jc == 0), stop=(jc == 1)),
                                inc=(st if (kc2 == 1 and jc == 1) else None))
                    ST_AC = st.n
                    if with_o:
                        M.wait("vector", st, ST_O)
                        for ic in range(2):
                            bi = P[0] if ic == 0 else P[2]
                            bI = P[1] if ic == 0 else P[3]
                            sct = cl * 2 + ic
                            M.use("vector", "obuf")
                            M.op("vector", lambda E, h=h, ic=ic, bI=bI: E.tensor_scalar(
                                out=obuf[:, :], in0=bI[:, :], scalar1=crossq[:, h, ic:ic + 1],
                                scalar2=None, op0=Op.mult))
                            v = M.op("vector", lambda E, bi=bi: E.tensor_tensor(
                                out=obuf[:, :], in0=obuf[:, :], in1=bi[:, :], op=Op.add), inc=sv)
                            M.free("P0" if ic == 0 else "P2", sv, v)
                            M.free("P1" if ic == 0 else "P3", sv, v)
                            M.wait("scalar", sv, v)
                            vsq = M.op("scalar", lambda E: E.square(osq[:, :], obuf[:, :]), inc=ss)
                            M.free("obuf", ss, vsq)
                            M.wait("vector", ss, vsq)
                            M.op("vector", lambda E: E.tensor_reduce(
                                out=osc[:, 0:1], in_=osq[:, :], axis=mybir.AxisListType.X, op=Op.add))
                            M.op("vector", lambda E: E.drain())
                            v = M.op("vector", lambda E: E.tensor_scalar(
                                out=osc[:, 1:2], in0=osc[:, 0:1], scalar1=1.0 / DV, scalar2=EPS,
                                op0=Op.mult, op1=Op.add), inc=sv)
                            M.wait("scalar", sv, v)
                            v = M.op("scalar", lambda E: E.sqrt(osc[:, 2:3], osc[:, 1:2]), inc=ss)
                            M.wait("vector", ss, v)
                            M.op("vector", lambda E: E.reciprocal(osc[:, 3:4], osc[:, 2:3]))
                            M.op("vector", lambda E: E.drain())
                            M.op("vector", lambda E, h=h, sct=sct: E.scalar_tensor_tensor(
                                out=ogat[:, sct, h * 512:h * 512 + 512], in0=obuf[:, :],
                                scalar=osc[:, 3:4], in1=gn_b[:, :], op0=Op.mult, op1=Op.mult), inc=sv)
                    M.wait("vector", st, ST_AC)
                    for kc2 in range(2):
                        M.op("vector", lambda E, h=h, kc2=kc2, g=GC[h]: E.scalar_tensor_tensor(
                            out=Sm[:, h, kc2, :], in0=Sm[:, h, kc2, :], scalar=g,
                            in1=P[4 + kc2][:, :], op0=Op.mult, op1=Op.add))
                    v = M.op("vector", lambda E, h=h: E.tensor_copy(
                        out=Sb[:, h, :, :].rearrange("p a b -> p (a b)"),
                        in_=Sm[:, h, :, :].rearrange("p a b -> p (a b)")), inc=sv)
                    M.free("P4", sv, v)
                    M.free("P5", sv, v)
                    M.free("Sb_w", sv, v)

        M.wait("tensor", dw, DW_P1)
        M.wait("tensor", sv, SM_ZERO)
        for tt in range(NTIL):
            emit_xfm_tile(tt)
            emit_proj_rope(tt, WkA, ktile, spill_kr, "ktile")
            emit_kcross(tt)
            emit_vproj(tt)
            emit_retention(tt, with_o=False)
            M.free("ktile", st, st.n)
            M.free("vtile", st, st.n)
            M.free("kctile", st, st.n)
        M.wait("sync", sv, sv.n)
        M.op("sync", lambda E: E.dma_start(
            out=ag_s_in[:, :].rearrange("(a p) v -> p a v", p=128),
            in_=Sm[:, :, :, :].rearrange("p h k v -> p (h k) v")), inc=do, k=16)
        if "p1dump" in dbg:
            d_sm = dout("dbg_sm", [128, H * 2 * DV])
            d_kt = dout("dbg_ktile", [128, 8 * TIL], BF16)
            d_vt = dout("dbg_vtile", [128, 4 * VDIM], BF16)
            d_kc = dout("dbg_kctile", [128, 4 * H * 256], BF16)
            d_xf = dout("dbg_xfm", [128, 8 * NT], BF16)
            M.wait("sync", sv, sv.n)
            M.op("sync", lambda E: E.dma_start(out=d_sm[:, :], in_=Sm[:, :, :, :].rearrange("p a b c -> p (a b c)")), inc=do, k=16)
            M.op("sync", lambda E: E.dma_start(out=d_kt[:, :], in_=ktile[:, :, :].rearrange("p a b -> p (a b)")), inc=do, k=16)
            M.op("sync", lambda E: E.dma_start(out=d_vt[:, :], in_=vtile[:, :, :].rearrange("p a b -> p (a b)")), inc=do, k=16)
            M.op("sync", lambda E: E.dma_start(out=d_kc[:, :], in_=kctile[:, :, :, :].rearrange("p a b c -> p (a b c)")), inc=do, k=16)
            M.op("sync", lambda E: E.dma_start(out=d_xf[:, :], in_=xfm[:, :, :].rearrange("p a b -> p (a b)")), inc=do, k=16)
            d_cs = dout("dbg_cossin", [128, 2 * NT], F16)
            M.op("sync", lambda E: E.dma_start(out=d_cs[:, :], in_=cossin[:, :, :].rearrange("p a b -> p (a b)")), inc=do, k=16)
        M.use("gpsimd", "ktile", "ktile_do")
        M.op("gpsimd", lambda E: E.dma_start(
            out=WkA[:, :, :],
            in_=Wq_i[:, :].rearrange("(a p) b -> p a b", p=128)), inc=dw, k=16)
        M.wait("gpsimd", do, do.n)
        M.op("gpsimd", lambda E: E.collective_compute(
            "AllGather", Op.bypass, replica_groups=[list(range(NCORES))],
            ins=[ag_s_in.opt()], outs=[ag_s_out.opt()]), inc=cc)

        def emit_s_gather():
            M.wait("gpsimd", cc, cc.n)
            for blk in range(8):
                h, kc2 = blk // 2, blk % 2
                M.op("gpsimd", lambda E, h=h, kc2=kc2, blk=blk: E.indirect_dma_start(
                    out=Sm[:, h, kc2, :], out_offset=None, in_=ag_s_out[:, :],
                    in_offset=IndirectOffsetOnAxis(ap=sinitr[:, blk:blk + 1], axis=0)), inc=dg, k=16)
            M.wait("vector", dg, dg.n)
            M.op("vector", lambda E: E.tensor_scalar(
                out=Sm[:, :, :, :].rearrange("p a b c -> p (a b c)"),
                in0=Sm[:, :, :, :].rearrange("p a b c -> p (a b c)"),
                scalar1=gate_s[:, 0:1], scalar2=None, op0=Op.mult))
            M.op("vector", lambda E: E.tensor_copy(
                out=Sb[:, :, :, :].rearrange("p a b c -> p (a b c)"),
                in_=Sm[:, :, :, :].rearrange("p a b c -> p (a b c)")), inc=sv)
            M.free("Sb_w", sv, sv.n)
        if "s8" in dbg:
            sdbg = dout("dbg_sinit", [128, H * 2 * DV])
            M.wait("sync", sv, sv.n)
            M.op("sync", lambda E, d=sdbg: E.dma_start(
                out=d[:, :], in_=Sm[:, :, :, :].rearrange("p a b c -> p (a b c)")), inc=do, k=16)

        if on("p2"):
            M.wait("tensor", dw, dw.n)
            DO_SPILL1 = do.n
            for tt in range(NTIL):
                T0 = tt * TIL
                M.wait("sync", do, DO_SPILL1)
                M.use("sync", "krt2", "vtile", "kctile")
                M.op("sync", lambda E, T0=T0: E.dma_start(
                    out=krt2[:, :, :],
                    in_=spill_kr[:, T0:T0 + TIL].rearrange("(a p) t -> p a t", p=128)), inc=dx, k=16)
                M.op("sync", lambda E, T0=T0: E.dma_start(
                    out=vtile[:, :, :],
                    in_=spill_v[T0:T0 + TIL, :].rearrange("(s p) d -> p s d", p=128)), inc=dx, k=16)
                for h in range(H):
                    M.op("sync", lambda E, T0=T0, h=h: E.dma_start(
                        out=kctile[:, :, h, :],
                        in_=spill_kc[h, T0:T0 + TIL, :].rearrange("(s p) k -> p s k", p=128)), inc=dx, k=16)
                DX_RELOAD = dx.n
                emit_proj_rope(tt, WkA, ktile, None, "ktile")
                if tt == 0:
                    emit_s_gather()
                M.wait("tensor", dx, DX_RELOAD)
                M.use("vector", "ogat")
                emit_retention(tt, with_o=True)
                M.wait("scalar", sv, sv.n)
                M.op("scalar", lambda E, T0=T0: E.dma_start(
                    out=spill_on[T0:T0 + TIL, :].rearrange("(s p) d -> p s d", p=128),
                    in_=ogat[:, :, :]), inc=do, k=16)
                M.free("ogat", do, do.n)
                M.free("ktile", st, st.n)
                M.free("krt2", st, st.n)
                M.free("vtile", st, st.n)
                M.free("kctile", st, st.n)

        if "p2dump" in dbg:
            d_at = dout("dbg_attnT", [128, 2 * CH], BF16)
            d_qt = dout("dbg_qrot", [128, 8 * TIL], BF16)
            M.wait("sync", sv, sv.n)
            M.op("sync", lambda E: E.dma_start(out=d_at[:, :], in_=attnT[:, :, :].rearrange("p a b -> p (a b)")), inc=do, k=16)
            M.op("sync", lambda E: E.dma_start(out=d_qt[:, :], in_=ktile[:, :, :].rearrange("p a b -> p (a b)")), inc=do, k=16)
            d_ob = dout("dbg_obuf", [128, DV])
            M.op("sync", lambda E: E.dma_start(out=d_ob[:, :], in_=obuf[:, :]), inc=do, k=16)
            d_osc = dout("dbg_osc", [128, 16])
            d_osq = dout("dbg_osq", [128, DV])
            M.op("sync", lambda E: E.dma_start(out=d_osc[:, :], in_=osc[:, :]), inc=do, k=16)
            M.op("sync", lambda E: E.dma_start(out=d_osq[:, :], in_=osq[:, :]), inc=do, k=16)
            d_on = dout("dbg_onorm", [NT, VDIM], BF16)
            M.wait("sync", do, do.n)
            for tt in range(NTIL):
                T0 = tt * TIL
                M.op("sync", lambda E, T0=T0: E.dma_start(
                    out=ogat[:, :, :],
                    in_=spill_on[T0:T0 + TIL, :].rearrange("(s p) d -> p s d", p=128)), inc=dx, k=16)
                M.wait("sync", dx, dx.n)
                M.op("sync", lambda E, T0=T0, d=d_on: E.dma_start(
                    out=d[T0:T0 + TIL, :].rearrange("(s p) d -> p s d", p=128),
                    in_=ogat[:, :, :]), inc=do, k=16)
                M.wait("sync", do, do.n)

        if on("a2b"):
            M.wait("gpsimd", st, st.n)
            M.wait("gpsimd", sv, sv.n)
            M.wait("gpsimd", do, do.n)
            M.wait("gpsimd", dx, dx.n)
            M.op("gpsimd", lambda E: E.dma_start(
                out=WgB[:, :, :],
                in_=Wg_i[:, :].rearrange("(a p) b -> p a b", p=128)), inc=dw, k=16)
            M.op("gpsimd", lambda E: E.dma_start(
                out=WoC[:, :, :],
                in_=Wo_i[:, :].rearrange("(a p) b -> p a b", p=128)), inc=dw, k=16)
            DW_A2B = dw.n
            M.wait("tensor", dw, DW_A2B)
            DO_ON = do.n
            if "t0dump" in dbg:
                d_pre0 = dout("dbg_pre0", [128, 1024])
                d_osc0 = dout("dbg_osc0", [128, 16])
            for tt in range(NTIL):
                T0 = tt * TIL
                M.wait("sync", do, DO_ON)
                M.use("sync", "ogat_b", "xin")
                M.op("sync", lambda E, T0=T0: E.dma_start(
                    out=ogat_b[:, :, :],
                    in_=spill_on[T0:T0 + TIL, :].rearrange("(s p) d -> p s d", p=128)), inc=dx, k=16)
                M.op("sync", lambda E, T0=T0: E.dma_start(
                    out=xin[:, :, :],
                    in_=x_loc[T0:T0 + TIL, :].rearrange("(s p) d -> p s d", p=128)), inc=dx, k=16)
                DX_T = dx.n
                M.use("scalar", "ogat2_w")
                for i, (sc, vc) in enumerate([(s, v) for s in range(4) for v in range(4)]):
                    bname = f"P{i % 4}"
                    bank = P[i % 4]
                    M.use("tensor", bname)
                    for dc in range(8):
                        M.op("tensor", lambda E, sc=sc, vc=vc, dc=dc, bank=bank, T0=T0: E.matmul(
                            bank[:, :], xfm[:, dc, T0 + sc * 128:T0 + sc * 128 + 128],
                            WgB[:, dc, vc * 512:vc * 512 + 512], start=(dc == 0), stop=(dc == 7)),
                            inc=(st if dc == 7 else None))
                    M.wait("scalar", st, st.n)
                    vs = M.op("scalar", lambda E, sc=sc, vc=vc, bank=bank: E.activation(
                        out=ogat2[:, sc, vc * 512:vc * 512 + 512], in_=bank[:, :], func=AF.Silu), inc=ss)
                    M.free(bname, ss, vs)
                    M.wait("vector", ss, vs)
                    if i == 0:
                        M.wait("vector", dx, DX_T)
                    M.op("vector", lambda E, sc=sc, vc=vc: E.tensor_tensor(
                        out=ogat2[:, sc, vc * 512:vc * 512 + 512],
                        in0=ogat2[:, sc, vc * 512:vc * 512 + 512],
                        in1=ogat_b[:, sc, vc * 512:vc * 512 + 512], op=Op.mult),
                        inc=(sv if i == 15 else None))
                GAT_DONE = sv.n
                M.free("ogat_b", sv, GAT_DONE)
                if A2B_SUB < 2:
                    continue
                for sc in range(4):
                    M.wait("tensor", sv, GAT_DONE)
                    for half in range(2):
                        pt = PTs[(2 * sc + half) % 2]
                        M.use("tensor", f"PT{(2 * sc + half) % 2}")
                        for fcl in range(8):
                            fc = half * 8 + fcl
                            M.op("tensor", lambda E, sc=sc, fc=fc, fcl=fcl, pt=pt: E.transpose(
                                pt[:, fcl * 128:fcl * 128 + 128],
                                ogat2[:, sc, fc * 128:fc * 128 + 128], ident_b[:, :]),
                                inc=(st if fcl == 7 else None))
                        M.wait("vector", st, st.n)
                        v = M.op("vector", lambda E, half=half, sc=sc, pt=pt: E.tensor_copy(
                            out=ogfm[:, half * 8:half * 8 + 8, sc * 128:sc * 128 + 128],
                            in_=pt[:, :].rearrange("p (a b) -> p a b", a=8)), inc=sv)
                        M.free(f"PT{(2 * sc + half) % 2}", sv, v)
                    OG_RDY = sv.n
                    M.wait("tensor", sv, OG_RDY)
                    M.use("tensor", "P4", "P5")
                    for dmc in range(2):
                        for fc in range(16):
                            M.op("tensor", lambda E, sc=sc, dmc=dmc, fc=fc: E.matmul(
                                P[4 + dmc][:, :], ogfm[:, fc, sc * 128:sc * 128 + 128],
                                WoC[:, fc, dmc * 512:dmc * 512 + 512],
                                start=(fc == 0), stop=(fc == 15)),
                                inc=(st if (fc == 15 and dmc == 1) else None))
                    M.wait("vector", st, st.n)
                    if A2B_SUB < 3:
                        continue
                    for dmc in range(2):
                        M.op("vector", lambda E, sc=sc, dmc=dmc: E.tensor_tensor(
                            out=x1pre[:, dmc * 512:dmc * 512 + 512], in0=P[4 + dmc][:, :],
                            in1=xin[:, sc, dmc * 512:dmc * 512 + 512], op=Op.add),
                            inc=(sv if dmc == 1 else None))
                    M.free("P4", sv, sv.n)
                    M.free("P5", sv, sv.n)
                    if "t0dump" in dbg and tt == 0 and sc == 1:
                        M.wait("sync", sv, sv.n)
                        M.op("sync", lambda E: E.dma_start(out=d_pre0[:, :], in_=x1pre[:, :]), inc=do, k=16)
                    vpre = M.op("vector", lambda E: E.tensor_reduce(
                        out=osc[:, 4:5], in_=x1pre[:, :], axis=mybir.AxisListType.X, op=Op.add), inc=sv)
                    M.wait("scalar", sv, vpre)
                    vsq = M.op("scalar", lambda E: E.square(osqb[:, :], x1pre[:, :]), inc=ss)
                    M.wait("vector", ss, vsq)
                    M.op("vector", lambda E: E.tensor_reduce(
                        out=osc[:, 5:6], in_=osqb[:, :], axis=mybir.AxisListType.X, op=Op.add))
                    M.op("vector", lambda E: E.drain())
                    M.op("vector", lambda E: E.tensor_scalar(
                        out=osc[:, 6:7], in0=osc[:, 4:5], scalar1=-1.0 / D_, scalar2=None, op0=Op.mult))
                    M.op("vector", lambda E: E.tensor_scalar(
                        out=osc[:, 7:8], in0=osc[:, 5:6], scalar1=1.0 / D_, scalar2=EPS, op0=Op.mult, op1=Op.add))
                    M.op("vector", lambda E: E.tensor_tensor(
                        out=osc[:, 8:9], in0=osc[:, 6:7], in1=osc[:, 6:7], op=Op.mult))
                    M.op("vector", lambda E: E.drain())
                    v = M.op("vector", lambda E: E.tensor_tensor(
                        out=osc[:, 10:11], in0=osc[:, 7:8], in1=osc[:, 8:9], op=Op.subtract), inc=sv)
                    M.op("vector", lambda E: E.drain())
                    M.wait("scalar", sv, v)
                    v = M.op("scalar", lambda E: E.sqrt(osc[:, 11:12], osc[:, 10:11]), inc=ss)
                    M.wait("vector", ss, v)
                    M.op("vector", lambda E: E.reciprocal(osc[:, 12:13], osc[:, 11:12]))
                    M.op("vector", lambda E: E.drain())
                    if "t0dump" in dbg and tt == 0 and sc == 1:
                        vz = M.op("vector", lambda E: E.tensor_copy(out=osc[:, 14:15], in_=osc[:, 12:13]), inc=sv)
                        M.wait("sync", sv, vz)
                        M.op("sync", lambda E: E.dma_start(out=d_osc0[:, :], in_=osc[:, :]), inc=do, k=16)
                        M.wait("vector", do, do.n)
                    M.op("vector", lambda E: E.tensor_scalar(
                        out=x1pre[:, :], in0=x1pre[:, :], scalar1=osc[:, 6:7], scalar2=osc[:, 12:13],
                        op0=Op.add, op1=Op.mult))
                    M.use("vector", "x1btile_d", "x1btile_t")
                    X1_RDY = M.op("vector", lambda E: E.tensor_copy(out=x1btile[:, :], in_=x1pre[:, :]), inc=sv)
                    M.wait("sync", sv, X1_RDY)
                    M.op("sync", lambda E, T0=T0, sc=sc: E.dma_start(
                        out=x1f_d[T0 + sc * 128:T0 + sc * 128 + 128, :], in_=x1pre[:, :]), inc=do, k=16)
                    M.free("x1pre_d", do, do.n)
                    M.op("sync", lambda E, T0=T0, sc=sc: E.dma_start(
                        out=x1b_in[T0 + sc * 128:T0 + sc * 128 + 128, :], in_=x1btile[:, :]), inc=do, k=16)
                    M.free("x1btile_d", do, do.n)
                    pt2 = PTs[(sc + 1) % 2]
                    M.use("tensor", f"PT{(sc + 1) % 2}", "x1fmt_r")
                    M.wait("tensor", sv, X1_RDY)
                    for dc in range(8):
                        M.op("tensor", lambda E, dc=dc, pt2=pt2: E.transpose(
                            pt2[:, dc * 128:dc * 128 + 128], x1btile[:, dc * 128:dc * 128 + 128],
                            ident_b[:, :]), inc=(st if dc == 7 else None))
                    M.free("x1btile_t", st, st.n)
                    M.wait("vector", st, st.n)
                    v = M.op("vector", lambda E, sc=sc, pt2=pt2: E.tensor_copy(
                        out=x1fmt[:, :, sc * 128:sc * 128 + 128],
                        in_=pt2[:, :].rearrange("p (a b) -> p a b", a=8)), inc=sv)
                    M.free(f"PT{(sc + 1) % 2}", sv, v)
                M.free("xin", sv, sv.n)
                M.free("ogat2_w", st, st.n)
                if A2B_SUB < 4:
                    continue
                M.wait("tensor", sv, sv.n)
                M.use("tensor", "P0")
                for dc in range(8):
                    M.op("tensor", lambda E, dc=dc: E.matmul(
                        P[0][0:16, :], wgate[:, dc, :], x1fmt[:, dc, :],
                        start=(dc == 0), stop=(dc == 7)), inc=(st if dc == 7 else None))
                M.free("x1fmt_r", st, st.n)
                M.wait("scalar", st, st.n)
                M.use("scalar", "obuf_l")
                v = M.op("scalar", lambda E: E.activation(out=explg[0:16, :], in_=P[0][0:16, :], func=AF.Exp), inc=ss)
                M.free("P0", ss, v)
                M.wait("tensor", ss, v)
                M.use("tensor", "P5")
                v = M.op("tensor", lambda E: E.matmul(
                    P[5][0:1, :], ones16[:, :], explg[0:16, :], start=True, stop=True), inc=st)
                M.wait("vector", st, v)
                v = M.op("vector", lambda E: E.tensor_copy(out=explg[32:33, :], in_=P[5][0:1, :]), inc=sv)
                M.free("P5", sv, v)
                M.wait("sync", sv, v)
                M.wait("sync", ss, ss.n)
                M.op("sync", lambda E, T0=T0: E.dma_start(
                    out=explog_in[:, T0:T0 + TIL], in_=explg[0:16, :]), inc=do, k=16)
                M.op("sync", lambda E, T0=T0: E.dma_start(
                    out=den_in[:, T0:T0 + TIL], in_=explg[32:33, :]), inc=do, k=16)
                M.free("obuf_l", do, do.n)
            if "x1out" in dbg:
                d_on0 = dout("dbg_on0", [TIL, VDIM], BF16)
                M.wait("sync", do, do.n)
                M.op("sync", lambda E: E.dma_start(
                    out=ogat2[:, :, :],
                    in_=spill_on[0:TIL, :].rearrange("(s p) d -> p s d", p=128)), inc=dx, k=16)
                M.wait("sync", dx, dx.n)
                M.op("sync", lambda E, d=d_on0: E.dma_start(
                    out=d[0:TIL, :].rearrange("(s p) d -> p s d", p=128),
                    in_=ogat2[:, :, :]), inc=do, k=16)
                d_og2 = dout("dbg_ogat2", [128, 4 * VDIM], BF16)
                d_ogb = dout("dbg_ogatb", [128, 4 * VDIM], BF16)
                M.op("sync", lambda E, d=d_ogb: E.dma_start(out=d[:, :], in_=ogat_b[:, :, :].rearrange("p a b -> p (a b)")), inc=do, k=16)
                d_wgb = dout("dbg_wgb", [128, 8 * 2048], BF16)
                M.op("sync", lambda E, d=d_wgb: E.dma_start(out=d[:, :], in_=WgB[:, :, :].rearrange("p a b -> p (a b)")), inc=do, k=16)
                d_ogfm = dout("dbg_ogfm", [128, 16 * TIL], BF16)
                d_x1p = dout("dbg_x1pre", [128, 1024])
                M.wait("sync", sv, sv.n)
                M.op("sync", lambda E, d=d_og2: E.dma_start(out=d[:, :], in_=ogat2[:, :, :].rearrange("p a b -> p (a b)")), inc=do, k=16)
                M.op("sync", lambda E, d=d_ogfm: E.dma_start(out=d[:, :], in_=ogfm[:, :, :].rearrange("p a b -> p (a b)")), inc=do, k=16)
                M.op("sync", lambda E, d=d_x1p: E.dma_start(out=d[:, :], in_=x1pre[:, :]), inc=do, k=16)
                x1dbg = dout("dbg_x1", [NT, D_])
                M.wait("sync", do, do.n)
                M.op("sync", lambda E, d=x1dbg: E.dma_start(out=d[:, :], in_=x1f_d[:, :]), inc=do, k=16)

        if on("broute"):
            M.wait("gpsimd", st, st.n)
            M.wait("gpsimd", sv, sv.n)
            M.wait("gpsimd", do, do.n)
            M.wait("gpsimd", dx, dx.n)
            M.op("gpsimd", lambda E: E.dma_start(
                out=WsgT[:, :, :],
                in_=WsgT_i[:, :].rearrange("(a p) b -> p a b", p=128)), inc=dw, k=16)
            M.op("gpsimd", lambda E: E.dma_start(
                out=WsuT[:, :, :],
                in_=WsuT_i[:, :].rearrange("(a p) b -> p a b", p=128)), inc=dw, k=16)
            M.op("gpsimd", lambda E: E.dma_start(
                out=WsdT[:, :, :],
                in_=WsdT_i[:, :].rearrange("(a p) b -> p a b", p=128)), inc=dw, k=16)
            DW_B = dw.n
            M.wait("gpsimd", do, do.n)
            M.op("gpsimd", lambda E: E.collective_compute(
                "AllGather", Op.bypass, replica_groups=[list(range(NCORES))],
                ins=[x1b_in.opt()], outs=[x1all.opt()]), inc=cc)
            M.op("gpsimd", lambda E: E.collective_compute(
                "AllGather", Op.bypass, replica_groups=[list(range(NCORES))],
                ins=[explog_in.opt()], outs=[explog_out.opt()]), inc=cc)
            M.op("gpsimd", lambda E: E.collective_compute(
                "AllGather", Op.bypass, replica_groups=[list(range(NCORES))],
                ins=[den_in.opt()], outs=[den_out.opt()]), inc=cc)
            CC_AG1 = cc.n
            M.wait("tensor", dw, DW_B)
            M.wait("sync", st, st.n)
            PRELUDE_DONE = [False]

            def emit_route_prelude():
                M.wait("gpsimd", cc, CC_AG1)
                for hh in range(2):
                    M.op("gpsimd", lambda E, hh=hh: E.indirect_dma_start(
                        out=scoresR[:, hh * 2048:hh * 2048 + 2048], out_offset=None,
                        in_=explog_out[:, :],
                        in_offset=IndirectOffsetOnAxis(ap=elr_s[:, hh:hh + 1], axis=0)), inc=dg, k=16)
                    M.op("gpsimd", lambda E, hh=hh: E.indirect_dma_start(
                        out=denT[:, hh * 2048:hh * 2048 + 2048], out_offset=None,
                        in_=den_out[:, :],
                        in_offset=IndirectOffsetOnAxis(ap=denr_s[:, hh:hh + 1], axis=0)), inc=dg, k=16)
                DG_SC = dg.n
                M.wait("vector", st, st.n)
                M.wait("vector", dg, DG_SC)
                M.op("vector", lambda E: E.reciprocal(denT[:, :], denT[:, :]))
                M.op("vector", lambda E: E.drain())
                SCORES_RDY = M.op("vector", lambda E: E.tensor_tensor(
                    out=scoresR[:, :], in0=scoresR[:, :], in1=denT[:, :], op=Op.mult), inc=sv)
                M.wait("sync", sv, SCORES_RDY)
                M.op("sync", lambda E: E.dma_start(
                    out=scores_d[:, :].rearrange("(a b) o -> a (b o)", a=8), in_=scoresR[:, :]), inc=do, k=16)
                M.op("vector", lambda E: E.memset(sc8[:, 0:1], 0.0))
                M.op("vector", lambda E: E.memset(sc8[:, 1:2], 1.0), inc=sv)
                PRELUDE_DONE[0] = True
                return do.n

            def emit_bis_iter():
                M.op("vector", lambda E: E.tensor_scalar(
                    out=sc8[:, 2:3], in0=sc8[:, 0:1], scalar1=sc8[:, 1:2], scalar2=0.5,
                    op0=Op.add, op1=Op.mult))
                M.op("vector", lambda E: E.drain())
                M.op("vector", lambda E: E.tensor_scalar(
                    out=mdump[:, :], in0=scoresR[:, :], scalar1=sc8[:, 2:3], scalar2=0.0,
                    op0=Op.is_ge, op1=Op.add, accum_out=sc8[:, 3:4]))
                M.op("vector", lambda E: E.drain())
                M.op("vector", lambda E: E.tensor_scalar(
                    out=sc8[:, 4:5], in0=sc8[:, 3:4], scalar1=float(CAP), scalar2=None, op0=Op.is_ge))
                M.op("vector", lambda E: E.tensor_tensor(
                    out=sc8[:, 6:7], in0=sc8[:, 2:3], in1=sc8[:, 0:1], op=Op.subtract))
                M.op("vector", lambda E: E.tensor_tensor(
                    out=sc8[:, 7:8], in0=sc8[:, 2:3], in1=sc8[:, 1:2], op=Op.subtract))
                M.op("vector", lambda E: E.tensor_scalar(
                    out=sc8[:, 5:6], in0=sc8[:, 4:5], scalar1=-1.0, scalar2=1.0, op0=Op.mult, op1=Op.add))
                M.op("vector", lambda E: E.scalar_tensor_tensor(
                    out=sc8[:, 0:1], in0=sc8[:, 6:7], scalar=sc8[:, 4:5], in1=sc8[:, 0:1],
                    op0=Op.mult, op1=Op.add))
                M.op("vector", lambda E: E.drain())
                M.op("vector", lambda E: E.scalar_tensor_tensor(
                    out=sc8[:, 1:2], in0=sc8[:, 7:8], scalar=sc8[:, 5:6], in1=sc8[:, 1:2],
                    op0=Op.mult, op1=Op.add), inc=sv)
                M.op("vector", lambda E: E.drain())

            def emit_b_tile(bt):
                T0 = bt * 256
                M.use("sync", "x1btm")
                M.op("sync", lambda E, T0=T0: E.dma_start(
                    out=x1btm[:, :, :],
                    in_=x1b_in[T0:T0 + 256, :].rearrange("(s p) d -> p s d", p=128)), inc=dx, k=16)
                M.wait("tensor", dx, dx.n)
                M.use("tensor", "x1fmb_r")
                for scb in range(2):
                    pt = PTs[scb]
                    M.use("tensor", f"PT{scb}")
                    for dc in range(8):
                        M.op("tensor", lambda E, scb=scb, dc=dc, pt=pt: E.transpose(
                            pt[:, dc * 128:dc * 128 + 128],
                            x1btm[:, scb, dc * 128:dc * 128 + 128], ident_b[:, :]),
                            inc=(st if dc == 7 else None))
                    M.wait("vector", st, st.n)
                    v = M.op("vector", lambda E, scb=scb, pt=pt: E.tensor_copy(
                        out=x1fmb[:, :, scb * 128:scb * 128 + 128],
                        in_=pt[:, :].rearrange("p (a b) -> p a b", a=8)), inc=sv)
                    M.free(f"PT{scb}", sv, v)
                FMB_RDY = sv.n
                M.free("x1btm", sv, FMB_RDY)
                M.wait("tensor", sv, FMB_RDY)
                M.use("scalar", "hsb_w")
                for fc in range(16):
                    pb = (fc % 2) * 2
                    M.use("tensor", f"P{pb}", f"P{pb + 1}")
                    for dc in range(8):
                        M.op("tensor", lambda E, fc=fc, dc=dc, pb=pb: E.matmul(
                            P[pb][:, 0:256], WsgT[:, dc, fc * 128:fc * 128 + 128],
                            x1fmb[:, dc, :], start=(dc == 0), stop=(dc == 7)))
                        M.op("tensor", lambda E, fc=fc, dc=dc, pb=pb: E.matmul(
                            P[pb + 1][:, 0:256], WsuT[:, dc, fc * 128:fc * 128 + 128],
                            x1fmb[:, dc, :], start=(dc == 0), stop=(dc == 7)),
                            inc=(st if dc == 7 else None))
                    M.wait("scalar", st, st.n)
                    vs = M.op("scalar", lambda E, fc=fc, pb=pb: E.activation(
                        out=hsb[:, fc, :], in_=P[pb][:, 0:256], func=AF.Gelu), inc=ss)
                    M.free(f"P{pb}", ss, vs)
                    M.wait("vector", ss, vs)
                    v = M.op("vector", lambda E, fc=fc, pb=pb: E.tensor_tensor(
                        out=hsb[:, fc, :], in0=hsb[:, fc, :], in1=P[pb + 1][:, 0:256], op=Op.mult), inc=sv)
                    M.free(f"P{pb + 1}", sv, v)
                HS_RDY = sv.n
                M.free("x1fmb_r", st, st.n)
                M.wait("tensor", sv, HS_RDY)
                M.use("sync", "x1frel")
                M.op("sync", lambda E, T0=T0: E.dma_start(
                    out=x1frel[:, :, :],
                    in_=x1f_d[T0:T0 + 256, :].rearrange("(s p) d -> p s d", p=128)), inc=dx, k=16)
                DX_X1F = dx.n
                for tcs in range(2):
                    for dmc in range(2):
                        bname = f"P{2 + tcs * 2 + dmc}"
                        bank = P[2 + tcs * 2 + dmc]
                        M.use("tensor", bname)
                        for fc in range(16):
                            M.op("tensor", lambda E, tcs=tcs, dmc=dmc, fc=fc, bank=bank: E.matmul(
                                bank[:, :], hsb[:, fc, tcs * 128:tcs * 128 + 128],
                                WsdT[:, fc, dmc * 512:dmc * 512 + 512],
                                start=(fc == 0), stop=(fc == 15)),
                                inc=(st if fc == 15 else None))
                ST_SH = st.n
                M.free("hsb_w", st, ST_SH)
                M.wait("vector", st, ST_SH)
                M.wait("vector", dx, DX_X1F)
                for tcs in range(2):
                    for dmc in range(2):
                        bank = P[2 + tcs * 2 + dmc]
                        M.op("vector", lambda E, tcs=tcs, dmc=dmc, bank=bank: E.tensor_tensor(
                            out=x1frel[:, tcs, dmc * 512:dmc * 512 + 512],
                            in0=x1frel[:, tcs, dmc * 512:dmc * 512 + 512],
                            in1=bank[:, :], op=Op.add),
                            inc=(sv if (tcs == 1 and dmc == 1) else None))
                COMB_RDY = sv.n
                for i in range(4):
                    M.free(f"P{2 + i}", sv, COMB_RDY)
                M.wait("gpsimd", sv, COMB_RDY)
                for tcs in range(2):
                    M.op("gpsimd", lambda E, bt=bt, tcs=tcs: E.indirect_dma_start(
                        out=acc_d[:, :],
                        out_offset=IndirectOffsetOnAxis(ap=accr_s[:, bt * 2 + tcs:bt * 2 + tcs + 1], axis=0),
                        in_=x1frel[:, tcs, :], in_offset=None), inc=dg, k=16)
                M.free("x1frel", dg, dg.n)

            bis_done = 0
            DO_SCORES = None
            for bt in range(8):
                emit_b_tile(bt)
                if bt == 3:
                    DO_SCORES = emit_route_prelude()
                if bt >= 4:
                    for _ in range(8):
                        if bis_done < BIS_IT:
                            emit_bis_iter()
                            bis_done += 1
            while bis_done < BIS_IT:
                emit_bis_iter()
                bis_done += 1
            DG_ACC0 = dg.n
            M.op("vector", lambda E: E.tensor_scalar(
                out=mdump[:, :], in0=scoresR[:, :], scalar1=sc8[:, 1:2], scalar2=0.0,
                op0=Op.is_ge, op1=Op.add, accum_out=sc8[:, 8:9]))
            M.op("vector", lambda E: E.drain())
            M.op("vector", lambda E: E.tensor_scalar(
                out=mA[:, :], in0=scoresR[:, :], scalar1=sc8[:, 1:2], scalar2=None, op0=Op.is_ge))
            v = M.op("vector", lambda E: E.tensor_scalar(
                out=mB[:, :], in0=scoresR[:, :], scalar1=sc8[:, 0:1], scalar2=None, op0=Op.is_ge), inc=sv)
            M.op("vector", lambda E: E.memset(onesS[:, :], 1.0))
            M.op("vector", lambda E: E.tensor_tensor(
                out=mB[:, :], in0=mB[:, :], in1=mA[:, :], op=Op.subtract))
            M.op("vector", lambda E: E.tensor_tensor_scan(
                out=rA[:, :], data0=onesS[:, :], data1=mA[:, :], initial=0.0,
                op0=Op.mult, op1=Op.add))
            M.op("vector", lambda E: E.drain())
            M.op("vector", lambda E: E.tensor_tensor(
                out=rA[:, :], in0=rA[:, :], in1=mA[:, :], op=Op.mult))
            M.op("vector", lambda E: E.tensor_tensor_scan(
                out=mA[:, :], data0=onesS[:, :], data1=mB[:, :], initial=0.0,
                op0=Op.mult, op1=Op.add))
            M.op("vector", lambda E: E.drain())
            M.op("vector", lambda E: E.drain())
            M.op("vector", lambda E: E.tensor_scalar(
                out=mA[:, :], in0=mA[:, :], scalar1=sc8[:, 8:9], scalar2=None, op0=Op.add))
            M.op("vector", lambda E: E.tensor_tensor(
                out=mA[:, :], in0=mA[:, :], in1=mB[:, :], op=Op.mult))
            M.op("vector", lambda E: E.drain())
            RC_RDY = M.op("vector", lambda E: E.tensor_tensor(
                out=onesS[:, :], in0=rA[:, :], in1=mA[:, :], op=Op.add), inc=sv)
            M.wait("sync", sv, RC_RDY)
            M.op("sync", lambda E: E.dma_start(out=rcomb_d[:, :], in_=onesS[:, :]), inc=do, k=16)
            DO_RC = do.n
            if "route" in dbg:
                tdbg = dout("dbg_sc8", [8, 12])
                M.wait("sync", sv, sv.n)
                M.op("sync", lambda E, d=tdbg: E.dma_start(out=d[:, :], in_=sc8[:, :]), inc=do, k=16)
            M.wait("gpsimd", do, DO_RC)
            for g in range(8):
                M.use("gpsimd", "rrep")
                vdg = M.op("gpsimd", lambda E, g=g: E.dma_start(
                    out=rrep[:, :], in_=rcomb_d[g:g + 1, :].to_broadcast([128, N_])), inc=dg, k=16)
                M.wait("vector", dg, vdg)
                v = M.op("vector", lambda E, g=g: E.max_index(
                    out=idxtab[:, g, :], in_max=findv[:, :], in_values=rrep[:, :]), inc=sv)
                M.free("rrep", sv, v)
                M.wait("gpsimd", sv, v)
                if g == 0:
                    M.wait("gpsimd", do, DO_SCORES)
                for j in range(4):
                    M.op("gpsimd", lambda E, g=g, j=j: E.indirect_dma_start(
                        out=wtab[:, g, j:j + 1], out_offset=None,
                        in_=scores_d[:, :],
                        in_offset=IndirectOffsetOnAxis(ap=idxtab[:, g, j:j + 1], axis=0),
                        element_offset=g * N_), inc=dg, k=16)
            DG_W = dg.n
            if "idx" in dbg:
                idbg = dout("dbg_idxtab", [128, 8 * 8], U32)
                wdbg = dout("dbg_wtab", [128, 8 * 4])
                M.wait("sync", dg, DG_W)
                M.wait("sync", sv, sv.n)
                M.op("sync", lambda E, d=idbg: E.dma_start(
                    out=d[:, :], in_=idxtab[:, :, :].rearrange("p a b -> p (a b)")), inc=do, k=16)
                M.op("sync", lambda E, d=wdbg: E.dma_start(
                    out=d[:, :], in_=wtab[:, :, :].rearrange("p a b -> p (a b)")), inc=do, k=16)

            M.wait("sync", sv, sv.n)
            M.op("sync", lambda E: E.dma_start(
                out=idxt_in[:, :].rearrange("(g p) j -> p g j", p=128),
                in_=idxtab[:, :, 0:4]), inc=do, k=16)
            M.wait("gpsimd", do, do.n)
            M.op("gpsimd", lambda E: E.collective_compute(
                "AllGather", Op.bypass, replica_groups=[list(range(NCORES))],
                ins=[idxt_in.opt()], outs=[idxt_all.opt()]), inc=cc)

        if on("c"):
            M.wait("gpsimd", dg, dg.n)
            for ei in range(E2):
                M.wait("gpsimd", st, st.n)
                M.op("gpsimd", lambda E, ei=ei: E.dma_start(
                    out=WgTe[:, :, :],
                    in_=WgTe_i[ei, :, :].rearrange("(a p) b -> p a b", p=128)), inc=dw, k=16)
                M.op("gpsimd", lambda E, ei=ei: E.dma_start(
                    out=WuTe[:, :, :],
                    in_=WuTe_i[ei, :, :].rearrange("(a p) b -> p a b", p=128)), inc=dw, k=16)
                M.op("gpsimd", lambda E, ei=ei: E.dma_start(
                    out=WdTe[:, :, :],
                    in_=WdTe_i[ei, :, :].rearrange("(a p) b -> p a b", p=128)), inc=dw, k=16)
                DW_E = dw.n
                M.wait("tensor", dw, DW_E)
                for bb in range(4):
                    g = 4 * ei + bb
                    M.use("vector", "xtokfm_w")
                    for j in range(4):
                        M.use("gpsimd", f"xtok{j % 2}")
                        vdg = M.op("gpsimd", lambda E, g=g, j=j, bb=bb: E.indirect_dma_start(
                            out=xtok[j % 2][:, :], out_offset=None, in_=x1all[:, :],
                            in_offset=IndirectOffsetOnAxis(ap=idxtab[:, g, j:j + 1], axis=0),
                            element_offset=bb * 4096 * 1024), inc=dg, k=16)
                        M.wait("tensor", dg, vdg)
                        pt = PTs[j % 2]
                        M.use("tensor", f"PT{j % 2}")
                        for dc in range(8):
                            M.op("tensor", lambda E, j=j, dc=dc, pt=pt: E.transpose(
                                pt[:, dc * 128:dc * 128 + 128],
                                xtok[j % 2][:, dc * 128:dc * 128 + 128], ident_b[:, :]),
                                inc=(st if dc == 7 else None))
                        M.free(f"xtok{j % 2}", st, st.n)
                        M.wait("vector", st, st.n)
                        v = M.op("vector", lambda E, j=j, pt=pt: E.tensor_copy(
                            out=xtokfm[:, :, j * 128:j * 128 + 128],
                            in_=pt[:, :].rearrange("p (a b) -> p a b", a=8)), inc=sv)
                        M.free(f"PT{j % 2}", sv, v)
                    XF_RDY = sv.n
                    M.wait("tensor", sv, XF_RDY)
                    M.use("scalar", "hfm_w")
                    for fc in range(16):
                        pb = (fc % 2) * 2
                        M.use("tensor", f"P{pb}", f"P{pb + 1}")
                        for dc in range(8):
                            M.op("tensor", lambda E, fc=fc, dc=dc, pb=pb: E.matmul(
                                P[pb][:, :], WgTe[:, dc, fc * 128:fc * 128 + 128],
                                xtokfm[:, dc, :], start=(dc == 0), stop=(dc == 7)))
                            M.op("tensor", lambda E, fc=fc, dc=dc, pb=pb: E.matmul(
                                P[pb + 1][:, :], WuTe[:, dc, fc * 128:fc * 128 + 128],
                                xtokfm[:, dc, :], start=(dc == 0), stop=(dc == 7)),
                                inc=(st if dc == 7 else None))
                        M.wait("scalar", st, st.n)
                        vs = M.op("scalar", lambda E, fc=fc, pb=pb: E.activation(
                            out=hfm[:, fc, :], in_=P[pb][:, :], func=AF.Gelu), inc=ss)
                        M.free(f"P{pb}", ss, vs)
                        M.wait("vector", ss, vs)
                        v = M.op("vector", lambda E, fc=fc, pb=pb: E.tensor_tensor(
                            out=hfm[:, fc, :], in0=hfm[:, fc, :], in1=P[pb + 1][:, :], op=Op.mult), inc=sv)
                        M.free(f"P{pb + 1}", sv, v)
                    H_RDY = sv.n
                    M.free("xtokfm_w", st, st.n)
                    M.wait("tensor", sv, H_RDY)
                    for tc in range(4):
                        for dmc in range(2):
                            bname = f"P{2 + dmc}"
                            bank = P[2 + dmc]
                            M.use("tensor", bname)
                            for fc in range(16):
                                M.op("tensor", lambda E, tc=tc, dmc=dmc, fc=fc, bank=bank: E.matmul(
                                    bank[:, :], hfm[:, fc, tc * 128:tc * 128 + 128],
                                    WdTe[:, fc, dmc * 512:dmc * 512 + 512],
                                    start=(fc == 0), stop=(fc == 15)),
                                    inc=(st if fc == 15 else None))
                        M.wait("vector", st, st.n)
                        M.use("vector", "eotile")
                        for dmc in range(2):
                            bank = P[2 + dmc]
                            v = M.op("vector", lambda E, tc=tc, dmc=dmc, bank=bank, g=g: E.tensor_scalar(
                                out=eotile[:, dmc * 512:dmc * 512 + 512], in0=bank[:, :],
                                scalar1=wtab[:, g, tc:tc + 1], scalar2=None, op0=Op.mult), inc=sv)
                            M.free(f"P{2 + dmc}", sv, v)
                        M.wait("sync", sv, sv.n)
                        M.op("sync", lambda E, g=g, tc=tc: E.dma_start(
                            out=eo_in[g * CAP + tc * 128:g * CAP + tc * 128 + 128, :],
                            in_=eotile[:, :]), inc=do, k=16)
                        M.free("eotile", do, do.n)
                    M.free("hfm_w", st, st.n)
            M.wait("gpsimd", do, do.n)
            M.op("gpsimd", lambda E: E.collective_compute(
                "AllGather", Op.bypass, replica_groups=[list(range(NCORES))],
                ins=[eo_in.opt()], outs=[eo_all.opt()]), inc=cc)
            CC_AG2 = cc.n
            M.wait("gpsimd", cc, CC_AG2)

        if on("d"):
            M.wait("gpsimd", st, st.n)
            M.wait("gpsimd", sv, sv.n)
            M.wait("gpsimd", do, do.n)
            DG_STAGE = dg.n
            for s in range(16):
                par = s % 2
                M.op("gpsimd", lambda E, s=s, par=par: E.indirect_dma_start(
                    out=idxsl[:, par * 4:par * 4 + 4], out_offset=None, in_=idxt_all[:, :],
                    in_offset=IndirectOffsetOnAxis(ap=idxr_s[:, s:s + 1], axis=0)), inc=dg, k=16)
                for j in range(4):
                    M.op("gpsimd", lambda E, s=s, j=j, par=par: E.indirect_dma_start(
                        out=eoD[par * 4 + j][:, :], out_offset=None, in_=eo_all[:, :],
                        in_offset=IndirectOffsetOnAxis(ap=eor_s[:, s * 4 + j:s * 4 + j + 1], axis=0)), inc=dg, k=16)
                DG_LOAD = dg.n
                M.wait("gpsimd", dg, DG_STAGE)
                M.wait("gpsimd", dg, DG_LOAD)
                for j in range(4):
                    M.op("gpsimd", lambda E, j=j, par=par: E.indirect_dma_start(
                        out=acc_d[:, :],
                        out_offset=IndirectOffsetOnAxis(ap=idxsl[:, par * 4 + j:par * 4 + j + 1], axis=0),
                        in_=eoD[par * 4 + j][:, :], in_offset=None,
                        compute_op=Op.add), inc=dg, k=16)
                DG_STAGE = dg.n
            for fc in range(16):
                M.use("gpsimd", "accf")
                vdg = M.op("gpsimd", lambda E, fc=fc: E.indirect_dma_start(
                    out=accf[:, :], out_offset=None, in_=acc_d[:, :],
                    in_offset=IndirectOffsetOnAxis(ap=accr_s[:, fc:fc + 1], axis=0)), inc=dg, k=16)
                M.wait("vector", dg, vdg)
                vpre = M.op("vector", lambda E: E.tensor_reduce(
                    out=osc[:, 4:5], in_=accf[:, :], axis=mybir.AxisListType.X, op=Op.add), inc=sv)
                M.wait("scalar", sv, vpre)
                vsq = M.op("scalar", lambda E: E.square(lnscr[:, :], accf[:, :]), inc=ss)
                M.wait("vector", ss, vsq)
                M.op("vector", lambda E: E.tensor_reduce(
                    out=osc[:, 5:6], in_=lnscr[:, :], axis=mybir.AxisListType.X, op=Op.add))
                M.op("vector", lambda E: E.drain())
                M.op("vector", lambda E: E.tensor_scalar(
                    out=osc[:, 6:7], in0=osc[:, 4:5], scalar1=-1.0 / D_, scalar2=None, op0=Op.mult))
                M.op("vector", lambda E: E.tensor_scalar(
                    out=osc[:, 7:8], in0=osc[:, 5:6], scalar1=1.0 / D_, scalar2=EPS, op0=Op.mult, op1=Op.add))
                M.op("vector", lambda E: E.tensor_tensor(
                    out=osc[:, 8:9], in0=osc[:, 6:7], in1=osc[:, 6:7], op=Op.mult))
                M.op("vector", lambda E: E.drain())
                v = M.op("vector", lambda E: E.tensor_tensor(
                    out=osc[:, 10:11], in0=osc[:, 7:8], in1=osc[:, 8:9], op=Op.subtract), inc=sv)
                M.op("vector", lambda E: E.drain())
                M.wait("scalar", sv, v)
                v = M.op("scalar", lambda E: E.sqrt(osc[:, 11:12], osc[:, 10:11]), inc=ss)
                M.wait("vector", ss, v)
                M.op("vector", lambda E: E.reciprocal(osc[:, 12:13], osc[:, 11:12]))
                M.op("vector", lambda E: E.drain())
                M.use("vector", "outt")
                v = M.op("vector", lambda E: E.tensor_scalar(
                    out=outt[:, :], in0=accf[:, :], scalar1=osc[:, 6:7], scalar2=osc[:, 12:13],
                    op0=Op.add, op1=Op.mult), inc=sv)
                M.free("accf", sv, v)
                M.wait("sync", sv, v)
                M.op("sync", lambda E, fc=fc: E.dma_start(
                    out=out_loc[fc * 128:fc * 128 + 128, :], in_=outt[:, :]), inc=do, k=16)
                M.free("outt", do, do.n)
        else:
            M.wait("vector", sv, sv.n)
            M.op("vector", lambda E: E.memset(outt[:, :], 0.0), inc=sv)
            M.wait("sync", sv, sv.n)
            for fc in range(16):
                M.op("sync", lambda E, fc=fc: E.dma_start(
                    out=out_loc[fc * 128:fc * 128 + 128, :], in_=outt[:, :]), inc=do, k=16)

        M.wait("gpsimd", do, do.n)
        M.wait("gpsimd", dg, dg.n)
        M.wait("gpsimd", dx, dx.n)
        M.wait("gpsimd", dw, dw.n)

        @block.sync
        def _(E):
            for f in M.q["sync"]:
                f(E)

        @block.vector
        def _(E):
            for f in M.q["vector"]:
                f(E)

        @block.scalar
        def _(E):
            for f in M.q["scalar"]:
                f(E)

        @block.tensor
        def _(E):
            for f in M.q["tensor"]:
                f(E)

        @block.gpsimd
        def _(E):
            for f in M.q["gpsimd"]:
                f(E)

    es.close()
    return nc, dbg_t


_BUILD_CACHE = {}


def _get_nc(upto="full", dbg=()):
    key = (upto, tuple(sorted(dbg)))
    if key not in _BUILD_CACHE:
        _BUILD_CACHE[key] = build_nc(upto, dbg)
    return _BUILD_CACHE[key]


def make_in_maps(inputs):
    sh = _host_shared()
    f32 = lambda a: np.ascontiguousarray(np.asarray(a), dtype=np.float32)
    x = f32(inputs["L0_attn_x"])
    Wq = f32(inputs["L0_attn_Wq"]); Wk = f32(inputs["L0_attn_Wk"])
    Wv = f32(inputs["L0_attn_Wv"]); Wg = f32(inputs["L0_attn_Wg"]); Wo = f32(inputs["L0_attn_Wo"])
    gn = f32(inputs["L0_attn_gn_weight"])
    gate_w = f32(inputs["L0_ffn_gate_w"])
    WgE = f32(inputs["L0_ffn_Wg"]); WuE = f32(inputs["L0_ffn_Wu"]); WdE = f32(inputs["L0_ffn_Wd"])
    Wsg = f32(inputs["L0_ffn_Ws_g"]); Wsu = f32(inputs["L0_ffn_Ws_u"]); Wsd = f32(inputs["L0_ffn_Ws_d"])

    shared = {
        "Wq": Wq, "Wk": Wk, "Wv": Wv, "Wg": Wg, "Wo": Wo,
        "gn_bc": np.ascontiguousarray(np.broadcast_to(np.tile(gn, 1)[None, :], (128, DV))),
        "gatewT": np.ascontiguousarray(gate_w.T),
        "WsgT": np.ascontiguousarray(Wsg.T), "WsuT": np.ascontiguousarray(Wsu.T),
        "WsdT": np.ascontiguousarray(Wsd.T),
        "ident": sh["ident"], "dmaskT": sh["dmaskT"],
        "crossk": sh["crossk"], "crossq": sh["crossq"],
        "ones16": sh["ones16"], "findvals": sh["findvals"],
    }
    in_maps = []
    for c in range(NCORES):
        b, hf = c // 2, c % 2
        pc = _host_percore(c)
        m = dict(shared)
        m["x_loc"] = np.ascontiguousarray(x[b, hf * NT:(hf + 1) * NT, :])
        m["cos_t"] = np.ascontiguousarray(sh["cos_full"][:, hf * NT:(hf + 1) * NT])
        m["sin_t"] = np.ascontiguousarray(sh["sin_full"][:, hf * NT:(hf + 1) * NT])
        m["WgTe"] = np.ascontiguousarray(np.stack([WgE[2 * c + i].T for i in range(E2)]))
        m["WuTe"] = np.ascontiguousarray(np.stack([WuE[2 * c + i].T for i in range(E2)]))
        m["WdTe"] = np.ascontiguousarray(np.stack([WdE[2 * c + i].T for i in range(E2)]))
        for k in ("gate", "sinit_rows", "explog_rows", "den_rows", "eo_rows",
                  "idx_rows", "acc_rows"):
            m[k] = pc[k]
        in_maps.append(m)
    return in_maps


def kernel(**inputs):
    nc, _ = _get_nc("full")
    in_maps = make_in_maps(inputs)
    res = run_bass_kernel_spmd(nc, in_maps, core_ids=list(range(NCORES)))
    out = np.zeros((B_, N_, D_), dtype=np.float32)
    for c in range(NCORES):
        b, hf = c // 2, c % 2
        out[b, hf * NT:(hf + 1) * NT, :] = res.results[c]["out_loc"]
    return out
